# version 03
# baseline (speedup 1.0000x reference)
import os
import numpy as np
import ml_dtypes
import concourse.bass as bass
import concourse.mybir as mybir
from concourse.bass_utils import run_bass_kernel_spmd

F32 = mybir.dt.float32
BF16 = mybir.dt.bfloat16
AF = mybir.ActivationFunctionType
ALU = mybir.AluOpType

TOK = 2048
D = 1024
FF = 4096
NE = 8
NDC = D // 128
NQC = TOK // 512
NFC = FF // 128
NH = 16
HD = 64
LN_EPS = 1e-5
FFN_LN_EPS = 1e-12


class Plan:
    def __init__(self, sems):
        self.sems = sems
        self.n = {k: 0 for k in sems}
        self.ops = {k: [] for k in ["T", "A", "V", "G", "S"]}
        self.bank_user = {}
        self.buf_user = {}

    def wait(self, eng, sem, val):
        if val <= 0:
            return
        s = self.sems[sem]
        self.ops[eng].append(lambda e, s=s, v=val: e.wait_ge(s, v))

    def op(self, eng, fn, inc=None, dma=False):
        if inc is None:
            self.ops[eng].append(fn)
            return None
        amt = 16 if dma else 1
        self.n[inc] += amt
        tgt = self.n[inc]
        s = self.sems[inc]
        self.ops[eng].append(lambda e, f=fn, s=s, a=amt: f(e).then_inc(s, a))
        return tgt

    def use_bank(self, eng, tag):
        if tag in self.bank_user:
            sem, cnt = self.bank_user[tag]
            self.wait(eng, sem, cnt)

    def set_bank(self, tag, sem, cnt):
        self.bank_user[tag] = (sem, cnt)

    def use_buf(self, eng, tag):
        if tag in self.buf_user:
            sem, cnt = self.buf_user[tag]
            self.wait(eng, sem, cnt)

    def set_buf(self, tag, sem, cnt):
        self.buf_user[tag] = (sem, cnt)


DEBUG = bool(int(os.environ.get("KDBG", "0")))


def build_program():
    nc = bass.Bass("TRN2", target_bir_lowering=False)

    xt = nc.dram_tensor("xt", [D, TOK], F32, kind="ExternalInput").ap()
    w1j = nc.dram_tensor("w1j", [NE, NFC, 128, D], F32, kind="ExternalInput").ap()
    w2j = nc.dram_tensor("w2j", [NE, 2, NDC, 128, 2048], F32, kind="ExternalInput").ap()
    b1s_d = nc.dram_tensor("b1s", [128, NE * NFC], F32, kind="ExternalInput").ap()
    b2s_d = nc.dram_tensor("b2s", [128, NE * NDC], F32, kind="ExternalInput").ap()
    wrts_d = nc.dram_tensor("wrts", [128, NDC * NE], F32, kind="ExternalInput").ap()
    rb_d = nc.dram_tensor("rb", [NE, 1], F32, kind="ExternalInput").ap()
    lnw_d = nc.dram_tensor("lnw", [128, NDC], F32, kind="ExternalInput").ap()
    lnb_d = nc.dram_tensor("lnb", [128, NDC], F32, kind="ExternalInput").ap()
    rots_d = nc.dram_tensor("rots", [128, NDC], F32, kind="ExternalInput").ap()
    ents_d = nc.dram_tensor("ents", [128, NDC], F32, kind="ExternalInput").ap()
    id16_d = nc.dram_tensor("id16", [128, 128], BF16, kind="ExternalInput").ap()
    id32_d = nc.dram_tensor("id32", [128, 128], F32, kind="ExternalInput").ap()
    ones_r_d = nc.dram_tensor("ones_r", [1, 128], BF16, kind="ExternalInput").ap()
    ones_c_d = nc.dram_tensor("ones_c", [128, 1], BF16, kind="ExternalInput").ap()
    epsc_d = nc.dram_tensor("epsc", [1, 2], F32, kind="ExternalInput").ap()
    yt = nc.dram_tensor("yt", [D, TOK], F32, kind="ExternalOutput").ap()
    xsp = nc.dram_tensor("xsp", [D, TOK], F32, kind="Internal").ap()
    cwd = nc.dram_tensor("cwd", [NE, TOK], BF16, kind="Internal").ap()
    dbg = {}
    if DEBUG:
        for nm, shp, dt in [("d_u1", [D, TOK], BF16), ("d_cw1", [NE, TOK], BF16),
                            ("d_s1", [D, TOK], F32), ("d_x1", [D, TOK], F32),
                            ("d_u2", [D, TOK], BF16), ("d_x2", [D, TOK], F32),
                            ("d_u3", [D, TOK], BF16), ("d_cw2", [NE, TOK], BF16),
                            ("d_lgtok", [128, 128], F32), ("d_cwtok", [128, 128], F32),
                            ("d_eq1", [128, 8], F32), ("d_eq2", [128, 8], F32),
                            ("d_tq1", [128, 8], F32), ("d_wc", [128, 4], F32),
                            ("d_ms", [128, 8], F32), ("d_lgT", [NE, TOK], F32),
                            ("d_cwb", [128, TOK], BF16), ("d_cwrow", [1, TOK], BF16),
                            ("d_s0", [D, TOK], F32), ("d_s1e", [D, TOK], F32),
                            ("d_cwbs", [NE, 128, TOK], BF16),
                            ("d_h0", [128, TOK], BF16), ("d_h15", [128, TOK], BF16),
                            ("d_ps2", [128, 512], F32), ("d_zt", [128, 512], F32),
                            ("d_u1b", [D, TOK], BF16), ("d_him", [128, 512], BF16)]:
            dbg[nm] = nc.dram_tensor(nm, shp, dt, kind="ExternalOutput").ap()

    base = [16896]

    def alloc(name, shape, dtype):
        nbytes = shape[-1] * (2 if dtype == BF16 else 4)
        nbytes = (nbytes + 31) & ~31
        t = nc.alloc_sbuf_tensor_at(name, shape, dtype, offset=base[0])
        base[0] += nbytes
        return t

    def alloc_at(name, shape, dtype, off):
        return nc.alloc_sbuf_tensor_at(name, shape, dtype, offset=off)

    R1 = base[0]
    X = [alloc_at(f"X{i}", [128, TOK], F32, R1 + i * 8192) for i in range(NDC)]
    H = [alloc_at(f"H{i}", [128, TOK], BF16, R1 + i * 4096) for i in range(16)]
    base[0] += 65536
    R2 = base[0]
    S = [alloc_at(f"S{i}", [128, TOK], F32, R2 + i * 8192) for i in range(NDC)]
    qT = [alloc_at(f"qT{i}", [128, TOK], BF16, R2 + i * 4096) for i in range(NDC)]
    kT = [alloc_at(f"kT{i}", [128, TOK], BF16, R2 + 32768 + i * 4096) for i in range(NDC)]
    mu_m = alloc_at("mu_m", [128, TOK], BF16, R2)
    rho_m = alloc_at("rho_m", [128, TOK], BF16, R2 + 4096)
    base[0] += 65536
    U = [alloc(f"U{i}", [128, TOK], BF16) for i in range(NDC)]
    W1s = [alloc(f"W1s{i}", [128, 1024], BF16) for i in range(2)]
    W2s = [alloc(f"W2s{i}", [128, 2048], BF16) for i in range(2)]
    PS = base[0]
    lgT = alloc_at("lgT", [NE, TOK], F32, PS)
    mu_e = alloc_at("mu_e", [128, TOK], BF16, PS)
    rho_e = alloc_at("rho_e", [128, TOK], BF16, PS + 4096)
    Vp = alloc_at("Vp", [128, 16 * 65], BF16, PS)
    At = [alloc_at(f"At{i}", [128, 512], BF16, PS + 2176 + i * 1024) for i in range(3)]
    rsb = alloc_at("rsb", [128, 512], BF16, PS + 2176 + 3 * 1024)
    sqs = alloc_at("sqs", [128, 512], BF16, PS + 2176 + 4 * 1024)
    cwrow = alloc_at("cwrow", [1, TOK], BF16, PS + 4096)
    base[0] += 8192
    cwb = alloc("cwb", [128, TOK], BF16)
    cwT = alloc("cwT", [NE, TOK], BF16)
    lgtok = alloc("lgtok", [128, 128], F32)
    msA = alloc("msA", [128, 128], F32)
    wc16 = alloc("wc16", [128, 48], F32)
    onesf = alloc("onesf", [128, 16], F32)
    zt = alloc("zt", [128, 512], F32)
    stb = [alloc(f"stb{i}", [128, 512], BF16) for i in range(2)]
    stq = [alloc(f"stq{i}", [128, 512], BF16) for i in range(2)]
    rowf = alloc("rowf", [1, 1536], F32)
    mu_row = alloc("mu_row", [1, 512], BF16)
    rho_row = alloc("rho_row", [1, 512], BF16)
    rs_row = alloc("rs_row", [1, 512], BF16)
    cells_b = alloc("cells_b", [1, 32], BF16)
    negc = alloc("negc", [128, 8], F32)
    b1s = alloc("b1s", [128, NE * NFC], F32)
    b2s = alloc("b2s", [128, NE * NDC], F32)
    wrts = alloc("wrts", [128, NDC * NE], BF16)
    rbc = alloc("rbc", [NE, 1], F32)
    lnw = alloc("lnw", [128, NDC], F32)
    lnb = alloc("lnb", [128, NDC], F32)
    rots = alloc("rots", [128, NDC], F32)
    ents = alloc("ents", [128, NDC], F32)
    id16 = alloc("id16", [128, 128], BF16)
    id32 = alloc("id32", [128, 128], F32)
    ones_r = alloc("ones_r", [1, 128], BF16)
    ones_c = alloc("ones_c", [128, 1], BF16)
    epsc = alloc("epsc", [1, 2], F32)
    nc.sbuf_base = base[0]
    assert base[0] <= 229000, f"sbuf overflow: {base[0]}"

    ps1 = [nc.alloc_psum_tensor(f"ps1{i}", [128, 512], F32) for i in range(2)]
    ps2 = [nc.alloc_psum_tensor(f"ps2{i}", [128, 512], F32) for i in range(2)]
    psa = nc.alloc_psum_tensor("psa", [128, 512], F32)
    psb = nc.alloc_psum_tensor("psb", [128, 512], F32)
    pso = nc.alloc_psum_tensor("pso", [128, 512], F32)
    psbf = nc.alloc_psum_tensor("psbf", [128, 1024], BF16)

    import contextlib
    stack = contextlib.ExitStack()
    semT = stack.enter_context(nc.semaphore("semT"))
    semA = stack.enter_context(nc.semaphore("semA"))
    semV = stack.enter_context(nc.semaphore("semV"))
    semG = stack.enter_context(nc.semaphore("semG"))
    semS = stack.enter_context(nc.semaphore("semS"))
    P = Plan({"T": semT, "A": semA, "V": semV, "G": semG, "S": semS})
    for _sm in (semT, semA, semV, semG, semS):
        nc.gpsimd.sem_clear(_sm)
    nc.all_engine_barrier()

    for i in range(NDC):
        P.op("S", lambda e, i=i: e.dma_start(out=X[i][:], in_=xt[i * 128:(i + 1) * 128, :]),
             inc="S", dma=True)
    for dst, src in [(b1s, b1s_d), (b2s, b2s_d), (rbc, rb_d), (lnw, lnw_d),
                     (lnb, lnb_d), (rots, rots_d), (ents, ents_d),
                     (id16, id16_d), (id32, id32_d), (ones_r, ones_r_d),
                     (ones_c, ones_c_d), (epsc, epsc_d)]:
        P.op("S", lambda e, d=dst, s=src: e.dma_start(out=d[:], in_=s), inc="S", dma=True)
    P.op("G", lambda e: e.dma_start(out=wrts[:], in_=wrts_d), inc="G", dma=True)
    n_loadS = P.n["S"]
    n_loadG = P.n["G"]

    state2 = {"last_cwrow_S": 0}

    def emit_router(u_ready_V):
        P.wait("T", "V", u_ready_V)
        for qc in range(NQC):
            c0 = qc * 512
            P.use_bank("T", "psa")
            nt = None
            for j in range(NDC):
                nt = P.op("T", lambda e, j=j, c0=c0: e.matmul(
                    psa[0:NE, :], wrts[:, j * NE:(j + 1) * NE], U[j][:, c0:c0 + 512],
                    start=(j == 0), stop=(j == 7)), inc="T" if j == 7 else None)
            P.wait("V", "T", nt)
            nv = P.op("V", lambda e, c0=c0: e.tensor_scalar_add(
                lgT[:, c0:c0 + 512], psa[0:NE, :], rbc[:]), inc="V")
            P.set_bank("psa", "V", nv)
        P.wait("T", "V", P.n["V"])
        P.use_bank("T", "psb")
        nt = None
        for t in range(16):
            nt = P.op("T", lambda e, t=t: e.transpose(
                psb[:, t * 8:(t + 1) * 8], lgT[:, t * 128:(t + 1) * 128], id32[0:NE, 0:NE]),
                inc="T" if t == 15 else None)
        P.wait("V", "T", nt)
        nv = P.op("V", lambda e: e.tensor_copy(lgtok[:], psb[:, 0:128]), inc="V")
        P.set_bank("psb", "V", nv)
        m1b = zt[:, 0:128]
        m2b = zt[:, 128:256]
        qA1 = zt[:, 256:384]
        qA2 = zt[:, 384:512]
        cwtok = zt[:, 0:128]
        P.op("V", lambda e: e.memset(onesf[:], 1.0))
        P.op("V", lambda e: e.memset(wc16[:], 0.0))
        for t in range(16):
            P.op("V", lambda e, t=t: e.max(msA[:, t * 8:(t + 1) * 8], lgtok[:, t * 8:(t + 1) * 8]))
        for t in range(16):
            P.op("V", lambda e, t=t: e.tensor_sub(
                wc16[:, t:t + 1], msA[:, t * 8 + 1:t * 8 + 2], msA[:, t * 8:t * 8 + 1]))
        nv = P.op("V", lambda e: e.tensor_copy(msA[:], msA[:]), inc="V")
        P.wait("A", "V", nv)
        na = P.op("A", lambda e: e.activation(wc16[:, 16:32], wc16[:, 0:16], AF.Sigmoid,
                                              scale=-1.0), inc="A")
        P.wait("V", "A", na)
        P.op("V", lambda e: e.tensor_sub(wc16[:, 32:48], onesf[:], wc16[:, 16:32]))
        for t in range(16):
            P.op("V", lambda e, t=t: e.tensor_scalar_mul(
                m1b[:, t * 8:(t + 1) * 8], onesf[:, 0:8], msA[:, t * 8:t * 8 + 1]))
        for t in range(16):
            P.op("V", lambda e, t=t: e.tensor_scalar_mul(
                m2b[:, t * 8:(t + 1) * 8], onesf[:, 0:8], msA[:, t * 8 + 1:t * 8 + 2]))
        P.op("V", lambda e: e.tensor_tensor(qA1[:], lgtok[:], m1b[:], ALU.is_equal))
        P.op("V", lambda e: e.tensor_tensor(qA2[:], lgtok[:], m2b[:], ALU.is_equal))
        P.op("V", lambda e: e.tensor_mul(m1b[:], qA1[:], qA2[:]))
        P.op("V", lambda e: e.tensor_sub(qA2[:], qA2[:], m1b[:]))
        for t in range(16):
            P.op("V", lambda e, t=t: e.tensor_scalar_mul(
                qA1[:, t * 8:(t + 1) * 8], qA1[:, t * 8:(t + 1) * 8], wc16[:, 16 + t:17 + t]))
        for t in range(16):
            P.op("V", lambda e, t=t: e.tensor_scalar_mul(
                qA2[:, t * 8:(t + 1) * 8], qA2[:, t * 8:(t + 1) * 8], wc16[:, 32 + t:33 + t]))
        P.op("V", lambda e: e.tensor_add(cwtok, qA1[:], qA2[:]), inc="V")
        P.wait("T", "V", P.n["V"])
        for g in range(4):
            P.use_bank("T", "psa")
            nt = None
            for k in range(4):
                t = g * 4 + k
                nt = P.op("T", lambda e, t=t, k=k: e.transpose(
                    psa[0:8, k * 128:(k + 1) * 128], zt[:, t * 8:t * 8 + 8],
                    id32[:, 0:128]), inc="T" if k == 3 else None)
            P.wait("V", "T", nt)
            if g == 0:
                P.wait("V", "S", state2.get("cw_dram_S", 0))
            nv = P.op("V", lambda e, g=g: e.tensor_copy(cwT[:, g * 512:(g + 1) * 512], psa[0:8, :]), inc="V")
            P.set_bank("psa", "V", nv)
        P.wait("S", "V", P.n["V"])
        ns = P.op("S", lambda e: e.dma_start(out=cwd, in_=cwT[:]), inc="S", dma=True)
        state2["cw_dram_S"] = ns
        return P.n["V"]

    def emit_ffn(spill_cnt, cw_ready_V):
        gelu_hist = []
        comb_hist = []
        prev_mm2_T = 0
        cwrow_T = 0
        cwb_evac_V = 0
        first_gelu = True
        for e_i in range(NE):
            P.wait("S", "T", cwrow_T)
            if e_i == 0:
                P.wait("S", "V", cw_ready_V)
            ns = P.op("S", lambda e, e_i=e_i: e.dma_start(
                out=cwrow[:], in_=cwd[e_i:e_i + 1, :]), inc="S", dma=True)
            state2["last_cwrow_S"] = ns
            P.wait("T", "S", ns)
            P.wait("T", "V", cwb_evac_V)
            for qc in range(NQC):
                bank = [psa, psb][qc % 2]
                tag = "psa" if qc % 2 == 0 else "psb"
                P.use_bank("T", tag)
                nt = P.op("T", lambda e, bank=bank, qc=qc: e.matmul(
                    bank[:], ones_r[:], cwrow[0:1, qc * 512:(qc + 1) * 512],
                    start=True, stop=True), inc="T")
                if qc == 3:
                    cwrow_T = nt
                P.wait("V", "T", nt)
                nv = P.op("V", lambda e, bank=bank, qc=qc: e.tensor_copy(
                    cwb[:, qc * 512:(qc + 1) * 512], bank[:]), inc="V")
                P.set_bank(tag, "V", nv)
                if qc == 3:
                    cwb_evac_V = nv
            if DEBUG and e_i == 1 and not state2.get("dumped_u1b") and state2.get("ffn_calls", 0) == 0:
                state2["dumped_u1b"] = True
                P.wait("S", "V", P.n["V"])
                for i_ in range(NDC):
                    P.op("S", lambda e, i_=i_: e.dma_start(
                        out=dbg["d_u1b"][i_ * 128:(i_ + 1) * 128, :], in_=U[i_][:]), inc="S", dma=True)
                P.wait("T", "S", P.n["S"])
            if DEBUG and not state2.get("dumped_cwbs_" + str(e_i)) and state2.get("ffn_calls", 0) == 0:
                state2["dumped_cwbs_" + str(e_i)] = True
                P.wait("S", "V", cwb_evac_V)
                P.op("S", lambda e, e_i=e_i: e.dma_start(out=dbg["d_cwbs"][e_i], in_=cwb[:]), inc="S", dma=True)
                P.wait("V", "S", P.n["S"])
            for fh in range(2):
                gelu_this_fh = None
                for fcl in range(16):
                    slot = fcl % 2
                    P.use_buf("G", ("w1", slot))
                    fcg = fh * 16 + fcl
                    ng = P.op("G", lambda e, e_i=e_i, fcg=fcg, slot=slot: e.dma_start(
                        out=W1s[slot][:], in_=w1j[e_i, fcg]), inc="G", dma=True)
                    P.wait("T", "G", ng)
                    for qc in range(NQC):
                        ci = fcl * 4 + qc
                        bank = ps1[ci % 2]
                        if len(gelu_hist) >= 2:
                            P.wait("T", "A", gelu_hist[-2])
                        nt = None
                        for j in range(NDC):
                            nt = P.op("T", lambda e, slot=slot, j=j, qc=qc, bank=bank: e.matmul(
                                bank[:], W1s[slot][:, j * 128:(j + 1) * 128],
                                U[j][:, qc * 512:(qc + 1) * 512],
                                start=(j == 0), stop=(j == 7)), inc="T" if j == 7 else None)
                        if qc == 3:
                            P.set_buf(("w1", slot), "T", nt)
                        if DEBUG and e_i == 1 and fh == 0 and fcl == 0 and qc == 0 and state2.get("ffn_calls", 0) == 0 and not state2.get("dumped_ps1"):
                            state2["dumped_ps1"] = True
                            P.wait("V", "T", nt)
                            nvd = P.op("V", lambda e, bank=bank: e.tensor_copy(zt[:], bank[:]), inc="V")
                            P.wait("S", "V", nvd)
                            P.op("S", lambda e: e.dma_start(out=dbg["d_zt"], in_=zt[:]), inc="S", dma=True)
                            P.wait("A", "S", P.n["S"])
                        P.wait("A", "T", nt)
                        if first_gelu:
                            P.wait("A", "S", spill_cnt)
                            first_gelu = False
                        if fcl == 0 and qc == 0 and prev_mm2_T:
                            P.wait("A", "T", prev_mm2_T)
                        bcol = e_i * NFC + fh * 16 + fcl
                        na = P.op("A", lambda e, bank=bank, fcl=fcl, qc=qc, bcol=bcol: e.activation(
                            H[fcl][:, qc * 512:(qc + 1) * 512], bank[:], AF.Gelu,
                            bias=b1s[:, bcol:bcol + 1], scale=1.0), inc="A")
                        gelu_hist.append(na)
                        gelu_this_fh = na
                        if DEBUG and e_i == 1 and fh == 0 and fcl == 0 and qc == 3 and state2.get("ffn_calls", 0) == 0 and not state2.get("dumped_himm"):
                            state2["dumped_himm"] = True
                            P.wait("S", "A", na)
                            P.op("S", lambda e: e.dma_start(out=dbg["d_him"], in_=H[0][:, 0:512]), inc="S", dma=True)
                            P.wait("A", "S", P.n["S"])
                for dc in range(NDC):
                    slot = dc % 2
                    P.use_buf("G", ("w2", slot))
                    ng = P.op("G", lambda e, e_i=e_i, fh=fh, dc=dc, slot=slot: e.dma_start(
                        out=W2s[slot][:], in_=w2j[e_i, fh, dc]), inc="G", dma=True)
                    P.wait("T", "G", ng)
                    for qc in range(NQC):
                        ci = dc * 4 + qc
                        bank = ps2[ci % 2]
                        if dc == 0 and qc == 0:
                            P.wait("T", "A", gelu_this_fh)
                        if len(comb_hist) >= 2:
                            P.wait("T", "V", comb_hist[-2])
                        nt = None
                        for fcl in range(16):
                            nt = P.op("T", lambda e, slot=slot, fcl=fcl, dc=dc, qc=qc, bank=bank: e.matmul(
                                bank[:], W2s[slot][:, fcl * 128:(fcl + 1) * 128],
                                H[fcl][:, qc * 512:(qc + 1) * 512],
                                start=(fcl == 0), stop=(fcl == 15)), inc="T" if fcl == 15 else None)
                        if qc == 3:
                            P.set_buf(("w2", slot), "T", nt)
                            if dc == 7:
                                prev_mm2_T = nt
                        P.wait("V", "T", nt)
                        c0 = qc * 512
                        if DEBUG and e_i == 1 and fh == 0 and dc == 0 and qc == 0 and state2.get("ffn_calls", 0) == 0:
                            nvd = P.op("V", lambda e, bank=bank: e.tensor_copy(zt[:], bank[:]), inc="V")
                            P.wait("S", "V", nvd)
                            P.op("S", lambda e: e.dma_start(out=dbg["d_ps2"], in_=zt[:]), inc="S", dma=True)
                            P.op("S", lambda e: e.dma_start(out=dbg["d_h0"], in_=H[0][:]), inc="S", dma=True)
                            P.op("S", lambda e: e.dma_start(out=dbg["d_h15"], in_=H[15][:]), inc="S", dma=True)
                            P.wait("V", "S", P.n["S"])
                        if fh == 0:
                            b2col = e_i * NDC + dc
                            P.op("V", lambda e, bank=bank, b2col=b2col: e.tensor_scalar_add(
                                zt[:], bank[:], b2s[:, b2col:b2col + 1]))
                            nv1 = P.op("V", lambda e, c0=c0: e.tensor_mul(
                                zt[:], zt[:], cwb[:, c0:c0 + 512]), inc="V")
                        else:
                            nv1 = P.op("V", lambda e, bank=bank, c0=c0: e.tensor_mul(
                                zt[:], bank[:], cwb[:, c0:c0 + 512]), inc="V")
                        comb_hist.append(nv1)
                        if e_i == 0 and fh == 0:
                            nv2 = P.op("V", lambda e, dc=dc, c0=c0: e.tensor_copy(
                                S[dc][:, c0:c0 + 512], zt[:]), inc="V")
                        else:
                            nv2 = P.op("V", lambda e, dc=dc, c0=c0: e.tensor_add(
                                S[dc][:, c0:c0 + 512], S[dc][:, c0:c0 + 512], zt[:]), inc="V")
            if DEBUG and e_i == 1 and not state2.get("dumped_s1e"):
                state2["dumped_s1e"] = True
                P.wait("S", "V", nv2)
                for i_ in range(NDC):
                    P.op("S", lambda e, i_=i_: e.dma_start(
                        out=dbg["d_s1e"][i_ * 128:(i_ + 1) * 128, :], in_=S[i_][:]), inc="S", dma=True)
                P.wait("V", "S", P.n["S"])
            if DEBUG and e_i == 0 and not state2.get("dumped_s0"):
                state2["dumped_s0"] = True
                P.wait("S", "V", nv2)
                for i_ in range(NDC):
                    P.op("S", lambda e, i_=i_: e.dma_start(
                        out=dbg["d_s0"][i_ * 128:(i_ + 1) * 128, :], in_=S[i_][:]), inc="S", dma=True)
                P.wait("V", "S", P.n["S"])
        return prev_mm2_T

    def emit_epilogue2(last_mm2_T):
        P.wait("S", "T", last_mm2_T)
        rel = None
        for i in range(NDC):
            rel = P.op("S", lambda e, i=i: e.dma_start(
                out=X[i][:], in_=xsp[i * 128:(i + 1) * 128, :]), inc="S", dma=True)
        for j in range(NDC):
            for qc in range(NQC):
                c0 = qc * 512
                P.op("V", lambda e, j=j, c0=c0: e.tensor_add(
                    S[j][:, c0:c0 + 512], S[j][:, c0:c0 + 512], U[j][:, c0:c0 + 512]), inc="V")
        emit_stats(S, mu_e, rho_e, FFN_LN_EPS)
        P.wait("V", "S", P.n["S"])
        chunks = [(j, c) for j in range(NDC) for c in range(16)]
        n_ch = len(chunks)

        def _sub(i):
            j, c = chunks[i]
            b = zt[:, (i % 4) * 128:(i % 4) * 128 + 128]
            P.op("V", lambda e, j=j, c=c, b=b: e.tensor_sub(
                b, S[j][:, c * 128:(c + 1) * 128], mu_e[:, c * 128:(c + 1) * 128]))

        def _mul(i):
            j, c = chunks[i]
            b = zt[:, (i % 4) * 128:(i % 4) * 128 + 128]
            P.op("V", lambda e, c=c, b=b: e.tensor_mul(b, b, rho_e[:, c * 128:(c + 1) * 128]))

        def _ts(i):
            j, c = chunks[i]
            b = zt[:, (i % 4) * 128:(i % 4) * 128 + 128]
            P.op("V", lambda e, j=j, b=b: e.tensor_scalar(
                b, b, lnw[:, j:j + 1], lnb[:, j:j + 1], ALU.mult, ALU.add))

        def _stt(i):
            j, c = chunks[i]
            b = zt[:, (i % 4) * 128:(i % 4) * 128 + 128]
            return P.op("V", lambda e, j=j, c=c, b=b: e.scalar_tensor_tensor(
                X[j][:, c * 128:(c + 1) * 128], b, 0.5, X[j][:, c * 128:(c + 1) * 128],
                ALU.mult, ALU.add), inc="V")

        last = None
        for i in range(n_ch):
            _sub(i)
            _mul(i)
            if i >= 1:
                _ts(i - 1)
            if i >= 2:
                last = _stt(i - 2)
        _ts(n_ch - 1)
        last = _stt(n_ch - 2)
        P.op("V", lambda e: e.tensor_add(onesf[:], onesf[:], onesf[:]))
        P.op("V", lambda e: e.memset(onesf[:], 1.0))
        last = _stt(n_ch - 1)
        return last

    def emit_stats(src, mu_t, rho_t, eps):
        for qc in range(NQC):
            c0 = qc * 512
            for j in range(NDC):
                sb, sq = stb[j % 2], stq[j % 2]
                P.use_buf("V", f"st{j % 2}")
                P.op("V", lambda e, s=src[j], sb=sb, c0=c0: e.tensor_copy(sb[:], s[:, c0:c0 + 512]))
                nv = P.op("V", lambda e, sb=sb, sq=sq: e.tensor_mul(sq[:], sb[:], sb[:]), inc="V")
                P.wait("T", "V", nv)
                if j == 0:
                    P.use_bank("T", "psa")
                    P.use_bank("T", "psb")
                P.op("T", lambda e, sb=sb, j=j: e.matmul(psa[0:1, :], ones_c[:], sb[:],
                                                         start=(j == 0), stop=(j == 7)))
                nt = P.op("T", lambda e, sq=sq, j=j: e.matmul(psb[0:1, :], ones_c[:], sq[:],
                                                              start=(j == 0), stop=(j == 7)), inc="T")
                P.set_buf(f"st{j % 2}", "T", nt)
            P.wait("V", "T", P.n["T"])
            r0 = rowf[0:1, 0:512]
            r1 = rowf[0:1, 512:1024]
            r2 = rowf[0:1, 1024:1536]
            P.op("V", lambda e: e.tensor_scalar_mul(r0, psa[0:1, :], 1.0 / D))
            P.op("V", lambda e: e.tensor_scalar_mul(r1, psb[0:1, :], 1.0 / D))
            P.op("V", lambda e: e.tensor_mul(r2, r0, r0))
            P.op("V", lambda e: e.tensor_copy(mu_row[:], r0))
            nv = P.op("V", lambda e: e.tensor_sub(r1, r1, r2), inc="V")
            P.wait("A", "V", nv)
            ecol = epsc[0:1, 0:1] if eps == LN_EPS else epsc[0:1, 1:2]
            P.op("A", lambda e, ecol=ecol: e.activation(r2, r1, AF.Ln, bias=ecol))
            na = P.op("A", lambda e: e.activation(rho_row[:], r2, AF.Exp, scale=-0.5), inc="A")
            P.wait("T", "A", na)
            P.use_bank("T", "psa")
            P.op("T", lambda e: e.matmul(psa[:], ones_r[:], mu_row[:], start=True, stop=True), inc="T")
            P.use_bank("T", "psb")
            n2 = P.op("T", lambda e: e.matmul(psb[:], ones_r[:], rho_row[:], start=True, stop=True), inc="T")
            P.wait("V", "T", n2)
            P.op("V", lambda e, mu_t=mu_t, c0=c0: e.tensor_copy(mu_t[:, c0:c0 + 512], psa[:]))
            nv = P.op("V", lambda e, rho_t=rho_t, c0=c0: e.tensor_copy(rho_t[:, c0:c0 + 512], psb[:]), inc="V")
            P.set_bank("psa", "V", nv)
            P.set_bank("psb", "V", nv)

    def emit_ln_u(src, mu_t, rho_t, eps, wait_S=0):
        if wait_S:
            P.wait("V", "S", wait_S)
        emit_stats(src, mu_t, rho_t, eps)
        last = None
        for j in range(NDC):
            for qc in range(NQC):
                c0 = qc * 512
                P.op("V", lambda e, s=src[j], mu_t=mu_t, c0=c0: e.tensor_sub(
                    zt[:], s[:, c0:c0 + 512], mu_t[:, c0:c0 + 512]))
                last = P.op("V", lambda e, j=j, rho_t=rho_t, c0=c0: e.tensor_mul(
                    U[j][:, c0:c0 + 512], zt[:], rho_t[:, c0:c0 + 512]), inc="V")
        return last

    def emit_attention(u_ready_V):
        P.wait("T", "V", u_ready_V)
        qk_v = None
        for t in range(NDC):
            P.op("V", lambda e, t=t: e.tensor_scalar_mul(qT[t][:], U[t][:], rots[:, t:t + 1]))
            qk_v = P.op("V", lambda e, t=t: e.tensor_scalar_mul(kT[t][:], U[t][:], ents[:, t:t + 1]), inc="V")
        P.wait("T", "V", qk_v)
        sq_bufs = [sqs, rsb]
        for t in range(NDC):
            for w, src in [(0, qT[t]), (1, kT[t])]:
                for qc in range(NQC):
                    i = (t * 2 + w) * 4 + qc
                    sb = sq_bufs[i % 2]
                    P.use_buf("V", f"sqb{i % 2}")
                    nv = P.op("V", lambda e, src=src, qc=qc, sb=sb: e.tensor_mul(
                        sb[:], src[:, qc * 512:(qc + 1) * 512], src[:, qc * 512:(qc + 1) * 512]), inc="V")
                    P.wait("T", "V", nv)
                    P.use_bank("T", "psa")
                    nt = P.op("T", lambda e, sb=sb: e.matmul(psa[0:1, :], ones_c[:], sb[:],
                                                             start=True, stop=True), inc="T")
                    P.set_buf(f"sqb{i % 2}", "T", nt)
                    P.wait("V", "T", nt)
                    nv = P.op("V", lambda e, i=i: e.reduce_max(
                        rowf[0:1, i:i + 1], psa[0:1, :], axis=mybir.AxisListType.X), inc="V")
                    P.set_bank("psa", "V", nv)
        for i in range(16):
            P.op("V", lambda e, i=i: e.reduce_max(
                rowf[0:1, 64 + i:65 + i], rowf[0:1, i * 4:(i + 1) * 4], axis=mybir.AxisListType.X))
        for t in range(NDC):
            P.op("V", lambda e, t=t: e.tensor_add(
                rowf[0:1, 96 + t:97 + t], rowf[0:1, 64 + 2 * t:65 + 2 * t],
                rowf[0:1, 65 + 2 * t:66 + 2 * t]))
        for t in range(NDC):
            nv = P.op("V", lambda e, t=t: e.tensor_scalar_mul(
                cells_b[0:1, t:t + 1], rowf[0:1, 96 + t:97 + t], -0.0625),
                inc="V" if t == 7 else None)
        P.wait("T", "V", nv)
        P.use_bank("T", "psb")
        nt = None
        for t in range(NDC):
            nt = P.op("T", lambda e, t=t: e.matmul(psb[:, t:t + 1], ones_r[:],
                                                   cells_b[0:1, t:t + 1], start=True, stop=True),
                      inc="T" if t == 7 else None)
        P.wait("V", "T", nt)
        nv = P.op("V", lambda e: e.tensor_copy(negc[:], psb[:, 0:8]), inc="V")
        P.set_bank("psb", "V", nv)
        negc_v = P.n["V"]
        P.wait("A", "V", negc_v)
        exp_hist = []
        for h in range(NH):
            tile = h // 2
            odd = h % 2
            p0 = odd * 64
            vst = 64 if odd else 65
            P.use_bank("T", "psbf")
            nt = None
            for kt in range(16):
                nt = P.op("T", lambda e, tile=tile, p0=p0, kt=kt: e.transpose(
                    psbf[:, kt * 64:(kt + 1) * 64],
                    U[tile][p0:p0 + 64, kt * 128:(kt + 1) * 128], id16[p0:p0 + 64, p0:p0 + 64]),
                    inc="T" if kt == 15 else None)
            P.wait("V", "T", nt)
            nv = None
            for kt in range(16):
                nv = P.op("V", lambda e, kt=kt, vst=vst: e.tensor_copy(
                    Vp[:, kt * vst:kt * vst + 64], psbf[:, kt * 64:(kt + 1) * 64]),
                    inc="V" if kt == 15 else None)
                if not odd:
                    P.op("V", lambda e, kt=kt: e.memset(Vp[:, kt * 65 + 64:kt * 65 + 65], 1.0))
            P.set_bank("psbf", "V", nv)
            vp_ready = nv
            for qc in range(NQC):
                c0 = qc * 512
                for kt in range(16):
                    bank = ps1[kt % 2]
                    if len(exp_hist) >= 2:
                        P.wait("T", "A", exp_hist[-2])
                    nt = P.op("T", lambda e, tile=tile, p0=p0, kt=kt, c0=c0, bank=bank: e.matmul(
                        bank[:], kT[tile][p0:p0 + 64, kt * 128:(kt + 1) * 128],
                        qT[tile][p0:p0 + 64, c0:c0 + 512], start=True, stop=True), inc="T")
                    P.wait("A", "T", nt)
                    at = At[kt % 3]
                    P.use_buf("A", f"At{kt % 3}")
                    na = P.op("A", lambda e, bank=bank, at=at, tile=tile: e.activation(
                        at[:], bank[:], AF.Exp, bias=negc[:, tile:tile + 1], scale=0.125), inc="A")
                    exp_hist.append(na)
                    P.wait("T", "A", na)
                    if kt == 0:
                        P.wait("T", "V", vp_ready)
                        P.use_bank("T", "pso")
                        if odd:
                            P.use_bank("T", "psa")
                    if odd:
                        nto = P.op("T", lambda e, kt=kt, at=at: e.matmul(
                            pso[64:128, :], Vp[:, kt * 64:(kt + 1) * 64], at[:],
                            start=(kt == 0), stop=(kt == 15), tile_position=(0, 64)), inc="T")
                        nto = P.op("T", lambda e, kt=kt, at=at: e.matmul(
                            psa[0:1, :], ones_c[:], at[:],
                            start=(kt == 0), stop=(kt == 15)), inc="T")
                    else:
                        nto = P.op("T", lambda e, kt=kt, at=at: e.matmul(
                            pso[0:65, :], Vp[:, kt * 65:(kt + 1) * 65], at[:],
                            start=(kt == 0), stop=(kt == 15)), inc="T")
                    P.set_buf(f"At{kt % 3}", "T", nto)
                P.wait("A", "T", P.n["T"])
                if odd:
                    P.op("A", lambda e: e.activation(rowf[0:1, 0:512], psa[0:1, :], AF.Ln))
                else:
                    P.op("A", lambda e: e.activation(rowf[0:1, 0:512], pso[64:65, :], AF.Ln))
                na = P.op("A", lambda e: e.activation(rs_row[:], rowf[0:1, 0:512], AF.Exp,
                                                      scale=-1.0), inc="A")
                if odd:
                    P.set_bank("psa", "A", na)
                P.wait("T", "A", na)
                P.use_bank("T", "psb")
                nt = P.op("T", lambda e: e.matmul(psb[:], ones_r[:], rs_row[:],
                                                  start=True, stop=True), inc="T")
                P.wait("V", "T", nt)
                P.op("V", lambda e, p0=p0: e.tensor_copy(rsb[p0:p0 + 64, :], psb[p0:p0 + 64, :]))
                P.op("V", lambda e, p0=p0: e.tensor_mul(
                    zt[p0:p0 + 64, :], pso[p0:p0 + 64, :], rsb[p0:p0 + 64, :]))
                nv = P.op("V", lambda e, tile=tile, p0=p0, c0=c0: e.tensor_add(
                    X[tile][p0:p0 + 64, c0:c0 + 512], X[tile][p0:p0 + 64, c0:c0 + 512],
                    zt[p0:p0 + 64, :]), inc="V")
                P.set_bank("pso", "V", nv)
                P.set_bank("psb", "V", nv)

    def dump_tiles(nm, tiles, wait_sem, wait_val):
        if not DEBUG:
            return
        P.wait("S", wait_sem, wait_val)
        for i, t_ in enumerate(tiles):
            P.op("S", lambda e, t_=t_, i=i, nm=nm: e.dma_start(
                out=dbg[nm][i * 128:(i + 1) * 128, :], in_=t_[:]), inc="S", dma=True)
        P.wait("V", "S", P.n["S"])

    def dump_row(nm, src, wait_sem, wait_val):
        if not DEBUG:
            return
        P.wait("S", wait_sem, wait_val)
        P.op("S", lambda e, src=src, nm=nm: e.dma_start(out=dbg[nm], in_=src[:]),
             inc="S", dma=True)
        P.wait("V", "S", P.n["S"])

    lnv = emit_ln_u(X, mu_m, rho_m, LN_EPS, wait_S=n_loadS)
    dump_tiles("d_u1", U, "V", lnv)
    P.wait("T", "G", n_loadG)
    cwv = emit_router(lnv)
    dump_row("d_cw1", cwT, "V", cwv)
    P.wait("S", "V", lnv)
    sp = None
    for i in range(NDC):
        sp = P.op("S", lambda e, i=i: e.dma_start(
            out=xsp[i * 128:(i + 1) * 128, :], in_=X[i][:]), inc="S", dma=True)
    last_mm2 = emit_ffn(sp, cwv)
    state2["ffn_calls"] = 1
    dump_tiles("d_s1", S, "V", P.n["V"])
    xu1 = emit_epilogue2(last_mm2)
    dump_tiles("d_x1", X, "V", xu1)
    lnv = emit_ln_u(X, mu_m, rho_m, LN_EPS)
    dump_tiles("d_u2", U, "V", lnv)
    emit_attention(lnv)
    dump_tiles("d_x2", X, "V", P.n["V"])
    lnv = emit_ln_u(X, mu_m, rho_m, LN_EPS)
    dump_tiles("d_u3", U, "V", lnv)
    cwv = emit_router(lnv)
    dump_row("d_cw2", cwT, "V", cwv)
    P.wait("S", "V", lnv)
    sp = None
    for i in range(NDC):
        sp = P.op("S", lambda e, i=i: e.dma_start(
            out=xsp[i * 128:(i + 1) * 128, :], in_=X[i][:]), inc="S", dma=True)
    last_mm2 = emit_ffn(sp, cwv)
    xupd = emit_epilogue2(last_mm2)
    P.wait("S", "V", xupd)
    for i in range(NDC):
        P.op("S", lambda e, i=i: e.dma_start(
            out=yt[i * 128:(i + 1) * 128, :], in_=X[i][:]), inc="S", dma=True)
    P.wait("S", "S", P.n["S"])

    with nc.Block() as block:
        @block.tensor
        def _(t):
            for f in P.ops["T"]:
                f(t)

        @block.scalar
        def _(a):
            for f in P.ops["A"]:
                f(a)

        @block.vector
        def _(v):
            for f in P.ops["V"]:
                f(v)

        @block.gpsimd
        def _(g):
            for f in P.ops["G"]:
                f(g)

        @block.sync
        def _(s):
            for f in P.ops["S"]:
                f(s)

    stack.close()
    return nc


_NC_CACHE = None


def kernel(L0_attn_inputs, L0_attn_rotation_params, L0_attn_entangle_params,
           L0_ffn_router_w, L0_ffn_router_b, L0_ffn_w1, L0_ffn_b1,
           L0_ffn_w2, L0_ffn_b2, L0_ffn_ln_w, L0_ffn_ln_b):
    global _NC_CACHE
    x = np.asarray(L0_attn_inputs, dtype=np.float32)
    rot = np.asarray(L0_attn_rotation_params, dtype=np.float32)
    ent = np.asarray(L0_attn_entangle_params, dtype=np.float32)
    wr = np.asarray(L0_ffn_router_w, dtype=np.float32)
    rb = np.asarray(L0_ffn_router_b, dtype=np.float32)
    w1 = np.asarray(L0_ffn_w1, dtype=np.float32)
    b1 = np.asarray(L0_ffn_b1, dtype=np.float32)
    w2 = np.asarray(L0_ffn_w2, dtype=np.float32)
    b2 = np.asarray(L0_ffn_b2, dtype=np.float32)
    lnw_a = np.asarray(L0_ffn_ln_w, dtype=np.float32)
    lnb_a = np.asarray(L0_ffn_ln_b, dtype=np.float32)

    w1_jit = np.ascontiguousarray(
        w1.reshape(NE, NFC, 128, NDC, 128).transpose(0, 1, 4, 3, 2)
    ).reshape(NE, NFC, 128, D)
    w2_jit = np.ascontiguousarray(
        w2.reshape(NE, NDC, 128, 2, 16, 128).transpose(0, 3, 1, 5, 4, 2)
    ).reshape(NE, 2, NDC, 128, 2048)
    b1s = np.ascontiguousarray(b1.reshape(NE, NFC, 128).transpose(2, 0, 1)).reshape(128, NE * NFC)
    b2s = np.ascontiguousarray(b2.reshape(NE, NDC, 128).transpose(2, 0, 1)).reshape(128, NE * NDC)
    wrts = np.ascontiguousarray(wr.T.reshape(NDC, 128, NE).transpose(1, 0, 2)).reshape(128, NDC * NE)
    lnw_s = np.ascontiguousarray(lnw_a.reshape(NDC, 128).T)
    lnb_s = np.ascontiguousarray(lnb_a.reshape(NDC, 128).T)
    rots_s = np.ascontiguousarray(rot.reshape(D).reshape(NDC, 128).T)
    ents_s = np.ascontiguousarray(ent.reshape(D).reshape(NDC, 128).T)
    rb_s = np.ascontiguousarray(rb.reshape(NE, 1))
    id16 = np.eye(128, dtype=ml_dtypes.bfloat16)
    id32 = np.eye(128, dtype=np.float32)
    ones_r = np.ones((1, 128), dtype=ml_dtypes.bfloat16)
    ones_c = np.ones((128, 1), dtype=ml_dtypes.bfloat16)

    if _NC_CACHE is None:
        _NC_CACHE = build_program()
    nc = _NC_CACHE

    common = {"w1j": w1_jit, "w2j": w2_jit, "b1s": b1s, "b2s": b2s,
              "wrts": wrts, "rb": rb_s, "lnw": lnw_s, "lnb": lnb_s,
              "rots": rots_s, "ents": ents_s, "id16": id16, "id32": id32,
              "ones_r": ones_r, "ones_c": ones_c,
              "epsc": np.array([[LN_EPS, FFN_LN_EPS]], dtype=np.float32)}
    in_maps = []
    for b in range(8):
        m = dict(common)
        m["xt"] = np.ascontiguousarray(x[b].T)
        in_maps.append(m)
    res = run_bass_kernel_spmd(nc, in_maps, core_ids=list(range(8)))
    out = np.empty((8, TOK, D), dtype=np.float32)
    for b in range(8):
        out[b] = res.results[b]["yt"].T
    if DEBUG:
        global DEBUG_OUT
        DEBUG_OUT = res.results
    return out


# version 04
# speedup vs baseline: 1.0316x; 1.0316x over previous
import os
import numpy as np
import ml_dtypes
import concourse.bass as bass
import concourse.mybir as mybir
from concourse.bass_utils import run_bass_kernel_spmd

F32 = mybir.dt.float32
BF16 = mybir.dt.bfloat16
AF = mybir.ActivationFunctionType
ALU = mybir.AluOpType

TOK = 2048
D = 1024
FF = 4096
NE = 8
NDC = D // 128
NQC = TOK // 512
NFC = FF // 128
NH = 16
HD = 64
LN_EPS = 1e-5
FFN_LN_EPS = 1e-12


class Plan:
    def __init__(self, sems):
        self.sems = sems
        self.n = {k: 0 for k in sems}
        self.ops = {k: [] for k in ["T", "A", "V", "G", "S"]}
        self.bank_user = {}
        self.buf_user = {}

    def wait(self, eng, sem, val):
        if val <= 0:
            return
        s = self.sems[sem]
        self.ops[eng].append(lambda e, s=s, v=val: e.wait_ge(s, v))

    def op(self, eng, fn, inc=None, dma=False):
        if inc is None:
            self.ops[eng].append(fn)
            return None
        amt = 16 if dma else 1
        self.n[inc] += amt
        tgt = self.n[inc]
        s = self.sems[inc]
        self.ops[eng].append(lambda e, f=fn, s=s, a=amt: f(e).then_inc(s, a))
        return tgt

    def use_bank(self, eng, tag):
        if tag in self.bank_user:
            sem, cnt = self.bank_user[tag]
            self.wait(eng, sem, cnt)

    def set_bank(self, tag, sem, cnt):
        self.bank_user[tag] = (sem, cnt)

    def use_buf(self, eng, tag):
        if tag in self.buf_user:
            sem, cnt = self.buf_user[tag]
            self.wait(eng, sem, cnt)

    def set_buf(self, tag, sem, cnt):
        self.buf_user[tag] = (sem, cnt)


DEBUG = bool(int(os.environ.get("KDBG", "0")))


def build_program():
    nc = bass.Bass("TRN2", target_bir_lowering=False)

    xt = nc.dram_tensor("xt", [D, TOK], F32, kind="ExternalInput").ap()
    w1j = nc.dram_tensor("w1j", [NE, NFC, 128, D], F32, kind="ExternalInput").ap()
    w2j = nc.dram_tensor("w2j", [NE, 2, NDC, 128, 2048], F32, kind="ExternalInput").ap()
    b1s_d = nc.dram_tensor("b1s", [128, NE * NFC], F32, kind="ExternalInput").ap()
    b2s_d = nc.dram_tensor("b2s", [128, NE * NDC], F32, kind="ExternalInput").ap()
    wrts_d = nc.dram_tensor("wrts", [128, NDC * NE], F32, kind="ExternalInput").ap()
    rb_d = nc.dram_tensor("rb", [NE, 1], F32, kind="ExternalInput").ap()
    lnw_d = nc.dram_tensor("lnw", [128, NDC], F32, kind="ExternalInput").ap()
    lnb_d = nc.dram_tensor("lnb", [128, NDC], F32, kind="ExternalInput").ap()
    rots_d = nc.dram_tensor("rots", [128, NDC], F32, kind="ExternalInput").ap()
    ents_d = nc.dram_tensor("ents", [128, NDC], F32, kind="ExternalInput").ap()
    id16_d = nc.dram_tensor("id16", [128, 128], BF16, kind="ExternalInput").ap()
    id32_d = nc.dram_tensor("id32", [128, 128], F32, kind="ExternalInput").ap()
    ones_r_d = nc.dram_tensor("ones_r", [1, 128], BF16, kind="ExternalInput").ap()
    ones_c_d = nc.dram_tensor("ones_c", [128, 1], BF16, kind="ExternalInput").ap()
    epsc_d = nc.dram_tensor("epsc", [1, 2], F32, kind="ExternalInput").ap()
    yt = nc.dram_tensor("yt", [D, TOK], F32, kind="ExternalOutput").ap()
    xsp = nc.dram_tensor("xsp", [D, TOK], F32, kind="Internal").ap()
    cwd = nc.dram_tensor("cwd", [NE, TOK], BF16, kind="Internal").ap()
    dbg = {}
    if DEBUG:
        for nm, shp, dt in [("d_u1", [D, TOK], BF16), ("d_cw1", [NE, TOK], BF16),
                            ("d_s1", [D, TOK], F32), ("d_x1", [D, TOK], F32),
                            ("d_u2", [D, TOK], BF16), ("d_x2", [D, TOK], F32),
                            ("d_u3", [D, TOK], BF16), ("d_cw2", [NE, TOK], BF16),
                            ("d_lgtok", [128, 128], F32), ("d_cwtok", [128, 128], F32),
                            ("d_eq1", [128, 8], F32), ("d_eq2", [128, 8], F32),
                            ("d_tq1", [128, 8], F32), ("d_wc", [128, 4], F32),
                            ("d_ms", [128, 8], F32), ("d_lgT", [NE, TOK], F32),
                            ("d_cwb", [128, TOK], BF16), ("d_cwrow", [1, TOK], BF16),
                            ("d_s0", [D, TOK], F32), ("d_s1e", [D, TOK], F32),
                            ("d_cwbs", [NE, 128, TOK], BF16),
                            ("d_h0", [128, TOK], BF16), ("d_h15", [128, TOK], BF16),
                            ("d_ps2", [128, 512], F32), ("d_zt", [128, 512], F32),
                            ("d_u1b", [D, TOK], BF16), ("d_him", [128, 512], BF16)]:
            dbg[nm] = nc.dram_tensor(nm, shp, dt, kind="ExternalOutput").ap()

    base = [16896]

    def alloc(name, shape, dtype):
        nbytes = shape[-1] * (2 if dtype == BF16 else 4)
        nbytes = (nbytes + 31) & ~31
        t = nc.alloc_sbuf_tensor_at(name, shape, dtype, offset=base[0])
        base[0] += nbytes
        return t

    def alloc_at(name, shape, dtype, off):
        return nc.alloc_sbuf_tensor_at(name, shape, dtype, offset=off)

    R1 = base[0]
    X = [alloc_at(f"X{i}", [128, TOK], F32, R1 + i * 8192) for i in range(NDC)]
    H = [alloc_at(f"H{i}", [128, TOK], BF16, R1 + i * 4096) for i in range(16)]
    base[0] += 65536
    R2 = base[0]
    S = [alloc_at(f"S{i}", [128, TOK], F32, R2 + i * 8192) for i in range(NDC)]
    qT = [alloc_at(f"qT{i}", [128, TOK], BF16, R2 + i * 4096) for i in range(NDC)]
    kT = [alloc_at(f"kT{i}", [128, TOK], BF16, R2 + 32768 + i * 4096) for i in range(NDC)]
    mu_m = alloc_at("mu_m", [128, TOK], BF16, R2)
    rho_m = alloc_at("rho_m", [128, TOK], BF16, R2 + 4096)
    base[0] += 65536
    U = [alloc(f"U{i}", [128, TOK], BF16) for i in range(NDC)]
    W1s = [alloc(f"W1s{i}", [128, 1024], BF16) for i in range(2)]
    W2s = [alloc(f"W2s{i}", [128, 2048], BF16) for i in range(2)]
    PS = base[0]
    lgT = alloc_at("lgT", [NE, TOK], F32, PS)
    mu_e = alloc_at("mu_e", [128, TOK], BF16, PS)
    rho_e = alloc_at("rho_e", [128, TOK], BF16, PS + 4096)
    Vp = alloc_at("Vp", [128, 16 * 65], BF16, PS)
    At = alloc_at("At", [128, 2048], BF16, PS + 2176)
    rsb = alloc_at("rsb", [128, 512], BF16, PS + 6272)
    sqs = alloc_at("sqs", [128, 1024], BF16, PS + 2176)
    cwrow = alloc_at("cwrow", [1, TOK], BF16, PS + 4096)
    base[0] += 8192
    cwb = alloc("cwb", [128, TOK], BF16)
    cwT = alloc("cwT", [NE, TOK], BF16)
    lgtok = alloc("lgtok", [128, 128], F32)
    msA = alloc("msA", [128, 128], F32)
    wc16 = alloc("wc16", [128, 48], F32)
    onesf = alloc("onesf", [128, 16], F32)
    zt = alloc("zt", [128, 512], F32)
    stb = [alloc(f"stb{i}", [128, 512], BF16) for i in range(2)]
    stq = [alloc(f"stq{i}", [128, 512], BF16) for i in range(2)]
    rowf = alloc("rowf", [1, 1536], F32)
    mu_row = alloc("mu_row", [1, 512], BF16)
    rho_row = alloc("rho_row", [1, 512], BF16)
    rs_row = alloc("rs_row", [1, 512], BF16)
    cells_b = alloc("cells_b", [1, 32], BF16)
    negc = alloc("negc", [128, 8], F32)
    b1s = alloc("b1s", [128, NE * NFC], F32)
    b2s = alloc("b2s", [128, NE * NDC], F32)
    wrts = alloc("wrts", [128, NDC * NE], BF16)
    rbc = alloc("rbc", [NE, 1], F32)
    lnw = alloc("lnw", [128, NDC], F32)
    lnb = alloc("lnb", [128, NDC], F32)
    rots = alloc("rots", [128, NDC], F32)
    ents = alloc("ents", [128, NDC], F32)
    id16 = alloc("id16", [128, 128], BF16)
    id32 = alloc("id32", [128, 128], F32)
    ones_r = alloc("ones_r", [1, 128], BF16)
    ones_c = alloc("ones_c", [128, 1], BF16)
    epsc = alloc("epsc", [1, 2], F32)
    nc.sbuf_base = base[0]
    assert base[0] <= 229000, f"sbuf overflow: {base[0]}"

    ps4 = nc.alloc_psum_tensor("ps4", [128, 2048], F32)
    ps1 = [ps4[:, 0:512], ps4[:, 512:1024]]
    ps2 = [ps4[:, 1024:1536], ps4[:, 1536:2048]]
    psa = nc.alloc_psum_tensor("psa", [128, 512], F32)
    psb = nc.alloc_psum_tensor("psb", [128, 512], F32)
    pso = nc.alloc_psum_tensor("pso", [128, 512], F32)
    psbf = nc.alloc_psum_tensor("psbf", [128, 1024], BF16)

    import contextlib
    stack = contextlib.ExitStack()
    semT = stack.enter_context(nc.semaphore("semT"))
    semA = stack.enter_context(nc.semaphore("semA"))
    semV = stack.enter_context(nc.semaphore("semV"))
    semG = stack.enter_context(nc.semaphore("semG"))
    semS = stack.enter_context(nc.semaphore("semS"))
    P = Plan({"T": semT, "A": semA, "V": semV, "G": semG, "S": semS})
    for _sm in (semT, semA, semV, semG, semS):
        nc.gpsimd.sem_clear(_sm)
    nc.all_engine_barrier()

    for i in range(NDC):
        P.op("S", lambda e, i=i: e.dma_start(out=X[i][:], in_=xt[i * 128:(i + 1) * 128, :]),
             inc="S", dma=True)
    for dst, src in [(b1s, b1s_d), (b2s, b2s_d), (rbc, rb_d), (lnw, lnw_d),
                     (lnb, lnb_d), (rots, rots_d), (ents, ents_d),
                     (id16, id16_d), (id32, id32_d), (ones_r, ones_r_d),
                     (ones_c, ones_c_d), (epsc, epsc_d)]:
        P.op("S", lambda e, d=dst, s=src: e.dma_start(out=d[:], in_=s), inc="S", dma=True)
    P.op("G", lambda e: e.dma_start(out=wrts[:], in_=wrts_d), inc="G", dma=True)
    n_loadS = P.n["S"]
    n_loadG = P.n["G"]

    state2 = {"last_cwrow_S": 0}

    def emit_router(u_ready_V):
        P.wait("T", "V", u_ready_V)
        for qc in range(NQC):
            c0 = qc * 512
            P.use_bank("T", "psa")
            nt = None
            for j in range(NDC):
                nt = P.op("T", lambda e, j=j, c0=c0: e.matmul(
                    psa[0:NE, :], wrts[:, j * NE:(j + 1) * NE], U[j][:, c0:c0 + 512],
                    start=(j == 0), stop=(j == 7)), inc="T" if j == 7 else None)
            P.wait("V", "T", nt)
            nv = P.op("V", lambda e, c0=c0: e.tensor_scalar_add(
                lgT[:, c0:c0 + 512], psa[0:NE, :], rbc[:]), inc="V")
            P.set_bank("psa", "V", nv)
        P.wait("T", "V", P.n["V"])
        P.use_bank("T", "psb")
        nt = None
        for t in range(16):
            nt = P.op("T", lambda e, t=t: e.transpose(
                psb[:, t * 8:(t + 1) * 8], lgT[:, t * 128:(t + 1) * 128], id32[0:NE, 0:NE]),
                inc="T" if t == 15 else None)
        P.wait("V", "T", nt)
        nv = P.op("V", lambda e: e.tensor_copy(lgtok[:], psb[:, 0:128]), inc="V")
        P.set_bank("psb", "V", nv)
        m1b = zt[:, 0:128]
        m2b = zt[:, 128:256]
        qA1 = zt[:, 256:384]
        qA2 = zt[:, 384:512]
        cwtok = zt[:, 0:128]
        P.op("V", lambda e: e.memset(onesf[:], 1.0))
        P.op("V", lambda e: e.memset(wc16[:], 0.0))
        for t in range(16):
            P.op("V", lambda e, t=t: e.max(msA[:, t * 8:(t + 1) * 8], lgtok[:, t * 8:(t + 1) * 8]))
        for t in range(16):
            P.op("V", lambda e, t=t: e.tensor_sub(
                wc16[:, t:t + 1], msA[:, t * 8 + 1:t * 8 + 2], msA[:, t * 8:t * 8 + 1]))
        nv = P.op("V", lambda e: e.tensor_copy(msA[:], msA[:]), inc="V")
        P.wait("A", "V", nv)
        na = P.op("A", lambda e: e.activation(wc16[:, 16:32], wc16[:, 0:16], AF.Sigmoid,
                                              scale=-1.0), inc="A")
        P.wait("V", "A", na)
        P.op("V", lambda e: e.tensor_sub(wc16[:, 32:48], onesf[:], wc16[:, 16:32]))
        for t in range(16):
            P.op("V", lambda e, t=t: e.tensor_scalar_mul(
                m1b[:, t * 8:(t + 1) * 8], onesf[:, 0:8], msA[:, t * 8:t * 8 + 1]))
        for t in range(16):
            P.op("V", lambda e, t=t: e.tensor_scalar_mul(
                m2b[:, t * 8:(t + 1) * 8], onesf[:, 0:8], msA[:, t * 8 + 1:t * 8 + 2]))
        P.op("V", lambda e: e.tensor_tensor(qA1[:], lgtok[:], m1b[:], ALU.is_equal))
        P.op("V", lambda e: e.tensor_tensor(qA2[:], lgtok[:], m2b[:], ALU.is_equal))
        P.op("V", lambda e: e.tensor_mul(m1b[:], qA1[:], qA2[:]))
        P.op("V", lambda e: e.tensor_sub(qA2[:], qA2[:], m1b[:]))
        for t in range(16):
            P.op("V", lambda e, t=t: e.tensor_scalar_mul(
                qA1[:, t * 8:(t + 1) * 8], qA1[:, t * 8:(t + 1) * 8], wc16[:, 16 + t:17 + t]))
        for t in range(16):
            P.op("V", lambda e, t=t: e.tensor_scalar_mul(
                qA2[:, t * 8:(t + 1) * 8], qA2[:, t * 8:(t + 1) * 8], wc16[:, 32 + t:33 + t]))
        P.op("V", lambda e: e.tensor_add(cwtok, qA1[:], qA2[:]), inc="V")
        P.wait("T", "V", P.n["V"])
        for g in range(4):
            P.use_bank("T", "psa")
            nt = None
            for k in range(4):
                t = g * 4 + k
                nt = P.op("T", lambda e, t=t, k=k: e.transpose(
                    psa[0:8, k * 128:(k + 1) * 128], zt[:, t * 8:t * 8 + 8],
                    id32[:, 0:128]), inc="T" if k == 3 else None)
            P.wait("V", "T", nt)
            if g == 0:
                P.wait("V", "S", state2.get("cw_dram_S", 0))
            nv = P.op("V", lambda e, g=g: e.tensor_copy(cwT[:, g * 512:(g + 1) * 512], psa[0:8, :]), inc="V")
            P.set_bank("psa", "V", nv)
        P.wait("S", "V", P.n["V"])
        ns = P.op("S", lambda e: e.dma_start(out=cwd, in_=cwT[:]), inc="S", dma=True)
        state2["cw_dram_S"] = ns
        return P.n["V"]

    def emit_ffn(spill_cnt, cw_ready_V):
        gelu_hist = []
        comb_hist = []
        prev_mm2_T = 0
        cwrow_T = 0
        cwb_evac_V = 0
        first_gelu = True
        for e_i in range(NE):
            P.wait("S", "T", cwrow_T)
            if e_i == 0:
                P.wait("S", "V", cw_ready_V)
            ns = P.op("S", lambda e, e_i=e_i: e.dma_start(
                out=cwrow[:], in_=cwd[e_i:e_i + 1, :]), inc="S", dma=True)
            state2["last_cwrow_S"] = ns
            P.wait("T", "S", ns)
            P.wait("T", "V", cwb_evac_V)
            for qc in range(NQC):
                bank = [psa, psb][qc % 2]
                tag = "psa" if qc % 2 == 0 else "psb"
                P.use_bank("T", tag)
                nt = P.op("T", lambda e, bank=bank, qc=qc: e.matmul(
                    bank[:], ones_r[:], cwrow[0:1, qc * 512:(qc + 1) * 512],
                    start=True, stop=True), inc="T")
                if qc == 3:
                    cwrow_T = nt
                P.wait("V", "T", nt)
                nv = P.op("V", lambda e, bank=bank, qc=qc: e.tensor_copy(
                    cwb[:, qc * 512:(qc + 1) * 512], bank[:]), inc="V")
                P.set_bank(tag, "V", nv)
                if qc == 3:
                    cwb_evac_V = nv
            if DEBUG and e_i == 1 and not state2.get("dumped_u1b") and state2.get("ffn_calls", 0) == 0:
                state2["dumped_u1b"] = True
                P.wait("S", "V", P.n["V"])
                for i_ in range(NDC):
                    P.op("S", lambda e, i_=i_: e.dma_start(
                        out=dbg["d_u1b"][i_ * 128:(i_ + 1) * 128, :], in_=U[i_][:]), inc="S", dma=True)
                P.wait("T", "S", P.n["S"])
            if DEBUG and not state2.get("dumped_cwbs_" + str(e_i)) and state2.get("ffn_calls", 0) == 0:
                state2["dumped_cwbs_" + str(e_i)] = True
                P.wait("S", "V", cwb_evac_V)
                P.op("S", lambda e, e_i=e_i: e.dma_start(out=dbg["d_cwbs"][e_i], in_=cwb[:]), inc="S", dma=True)
                P.wait("V", "S", P.n["S"])
            for fh in range(2):
                gelu_this_fh = None
                for fcl in range(16):
                    slot = fcl % 2
                    P.use_buf("G", ("w1", slot))
                    fcg = fh * 16 + fcl
                    ng = P.op("G", lambda e, e_i=e_i, fcg=fcg, slot=slot: e.dma_start(
                        out=W1s[slot][:], in_=w1j[e_i, fcg]), inc="G", dma=True)
                    P.wait("T", "G", ng)
                    for qc in range(NQC):
                        ci = fcl * 4 + qc
                        bank = ps1[ci % 2]
                        if len(gelu_hist) >= 2:
                            P.wait("T", "A", gelu_hist[-2])
                        nt = None
                        for j in range(NDC):
                            nt = P.op("T", lambda e, slot=slot, j=j, qc=qc, bank=bank: e.matmul(
                                bank, W1s[slot][:, j * 128:(j + 1) * 128],
                                U[j][:, qc * 512:(qc + 1) * 512],
                                start=(j == 0), stop=(j == 7)), inc="T" if j == 7 else None)
                        if qc == 3:
                            P.set_buf(("w1", slot), "T", nt)
                        if DEBUG and e_i == 1 and fh == 0 and fcl == 0 and qc == 0 and state2.get("ffn_calls", 0) == 0 and not state2.get("dumped_ps1"):
                            state2["dumped_ps1"] = True
                            P.wait("V", "T", nt)
                            nvd = P.op("V", lambda e, bank=bank: e.tensor_copy(zt[:], bank), inc="V")
                            P.wait("S", "V", nvd)
                            P.op("S", lambda e: e.dma_start(out=dbg["d_zt"], in_=zt[:]), inc="S", dma=True)
                            P.wait("A", "S", P.n["S"])
                        P.wait("A", "T", nt)
                        if first_gelu:
                            P.wait("A", "S", spill_cnt)
                            first_gelu = False
                        if fcl == 0 and qc == 0 and prev_mm2_T:
                            P.wait("A", "T", prev_mm2_T)
                        bcol = e_i * NFC + fh * 16 + fcl
                        na = P.op("A", lambda e, bank=bank, fcl=fcl, qc=qc, bcol=bcol: e.activation(
                            H[fcl][:, qc * 512:(qc + 1) * 512], bank, AF.Gelu,
                            bias=b1s[:, bcol:bcol + 1], scale=1.0), inc="A")
                        gelu_hist.append(na)
                        gelu_this_fh = na
                        if DEBUG and e_i == 1 and fh == 0 and fcl == 0 and qc == 3 and state2.get("ffn_calls", 0) == 0 and not state2.get("dumped_himm"):
                            state2["dumped_himm"] = True
                            P.wait("S", "A", na)
                            P.op("S", lambda e: e.dma_start(out=dbg["d_him"], in_=H[0][:, 0:512]), inc="S", dma=True)
                            P.wait("A", "S", P.n["S"])
                for dc in range(NDC):
                    slot = dc % 2
                    P.use_buf("G", ("w2", slot))
                    ng = P.op("G", lambda e, e_i=e_i, fh=fh, dc=dc, slot=slot: e.dma_start(
                        out=W2s[slot][:], in_=w2j[e_i, fh, dc]), inc="G", dma=True)
                    P.wait("T", "G", ng)
                    for qc in range(NQC):
                        ci = dc * 4 + qc
                        bank = ps2[ci % 2]
                        if dc == 0 and qc == 0:
                            P.wait("T", "A", gelu_this_fh)
                        if len(comb_hist) >= 2:
                            P.wait("T", "V", comb_hist[-2])
                        nt = None
                        for fcl in range(16):
                            nt = P.op("T", lambda e, slot=slot, fcl=fcl, dc=dc, qc=qc, bank=bank: e.matmul(
                                bank, W2s[slot][:, fcl * 128:(fcl + 1) * 128],
                                H[fcl][:, qc * 512:(qc + 1) * 512],
                                start=(fcl == 0), stop=(fcl == 15)), inc="T" if fcl == 15 else None)
                        if qc == 3:
                            P.set_buf(("w2", slot), "T", nt)
                            if dc == 7:
                                prev_mm2_T = nt
                        P.wait("V", "T", nt)
                        c0 = qc * 512
                        if DEBUG and e_i == 1 and fh == 0 and dc == 0 and qc == 0 and state2.get("ffn_calls", 0) == 0:
                            nvd = P.op("V", lambda e, bank=bank: e.tensor_copy(zt[:], bank), inc="V")
                            P.wait("S", "V", nvd)
                            P.op("S", lambda e: e.dma_start(out=dbg["d_ps2"], in_=zt[:]), inc="S", dma=True)
                            P.op("S", lambda e: e.dma_start(out=dbg["d_h0"], in_=H[0][:]), inc="S", dma=True)
                            P.op("S", lambda e: e.dma_start(out=dbg["d_h15"], in_=H[15][:]), inc="S", dma=True)
                            P.wait("V", "S", P.n["S"])
                        if fh == 0:
                            b2col = e_i * NDC + dc
                            P.op("V", lambda e, bank=bank, b2col=b2col: e.tensor_scalar_add(
                                zt[:], bank, b2s[:, b2col:b2col + 1]))
                            nv1 = P.op("V", lambda e, c0=c0: e.tensor_mul(
                                zt[:], zt[:], cwb[:, c0:c0 + 512]), inc="V")
                        else:
                            nv1 = P.op("V", lambda e, bank=bank, c0=c0: e.tensor_mul(
                                zt[:], bank, cwb[:, c0:c0 + 512]), inc="V")
                        comb_hist.append(nv1)
                        if e_i == 0 and fh == 0:
                            nv2 = P.op("V", lambda e, dc=dc, c0=c0: e.tensor_copy(
                                S[dc][:, c0:c0 + 512], zt[:]), inc="V")
                        else:
                            nv2 = P.op("V", lambda e, dc=dc, c0=c0: e.tensor_add(
                                S[dc][:, c0:c0 + 512], S[dc][:, c0:c0 + 512], zt[:]), inc="V")
            if DEBUG and e_i == 1 and not state2.get("dumped_s1e"):
                state2["dumped_s1e"] = True
                P.wait("S", "V", nv2)
                for i_ in range(NDC):
                    P.op("S", lambda e, i_=i_: e.dma_start(
                        out=dbg["d_s1e"][i_ * 128:(i_ + 1) * 128, :], in_=S[i_][:]), inc="S", dma=True)
                P.wait("V", "S", P.n["S"])
            if DEBUG and e_i == 0 and not state2.get("dumped_s0"):
                state2["dumped_s0"] = True
                P.wait("S", "V", nv2)
                for i_ in range(NDC):
                    P.op("S", lambda e, i_=i_: e.dma_start(
                        out=dbg["d_s0"][i_ * 128:(i_ + 1) * 128, :], in_=S[i_][:]), inc="S", dma=True)
                P.wait("V", "S", P.n["S"])
        return prev_mm2_T

    def emit_epilogue2(last_mm2_T):
        P.wait("S", "T", last_mm2_T)
        rel = None
        for i in range(NDC):
            rel = P.op("S", lambda e, i=i: e.dma_start(
                out=X[i][:], in_=xsp[i * 128:(i + 1) * 128, :]), inc="S", dma=True)
        for j in range(NDC):
            for qc in range(NQC):
                c0 = qc * 512
                P.op("V", lambda e, j=j, c0=c0: e.tensor_add(
                    S[j][:, c0:c0 + 512], S[j][:, c0:c0 + 512], U[j][:, c0:c0 + 512]), inc="V")
        emit_stats(S, mu_e, rho_e, FFN_LN_EPS)
        P.wait("V", "S", P.n["S"])
        chunks = [(j, c) for j in range(NDC) for c in range(16)]
        n_ch = len(chunks)

        def _sub(i):
            j, c = chunks[i]
            b = zt[:, (i % 4) * 128:(i % 4) * 128 + 128]
            P.op("V", lambda e, j=j, c=c, b=b: e.tensor_sub(
                b, S[j][:, c * 128:(c + 1) * 128], mu_e[:, c * 128:(c + 1) * 128]))

        def _mul(i):
            j, c = chunks[i]
            b = zt[:, (i % 4) * 128:(i % 4) * 128 + 128]
            P.op("V", lambda e, c=c, b=b: e.tensor_mul(b, b, rho_e[:, c * 128:(c + 1) * 128]))

        def _ts(i):
            j, c = chunks[i]
            b = zt[:, (i % 4) * 128:(i % 4) * 128 + 128]
            P.op("V", lambda e, j=j, b=b: e.tensor_scalar(
                b, b, lnw[:, j:j + 1], lnb[:, j:j + 1], ALU.mult, ALU.add))

        def _stt(i):
            j, c = chunks[i]
            b = zt[:, (i % 4) * 128:(i % 4) * 128 + 128]
            return P.op("V", lambda e, j=j, c=c, b=b: e.scalar_tensor_tensor(
                X[j][:, c * 128:(c + 1) * 128], b, 0.5, X[j][:, c * 128:(c + 1) * 128],
                ALU.mult, ALU.add), inc="V")

        last = None
        for i in range(n_ch):
            _sub(i)
            _mul(i)
            if i >= 1:
                _ts(i - 1)
            if i >= 2:
                last = _stt(i - 2)
        _ts(n_ch - 1)
        last = _stt(n_ch - 2)
        P.op("V", lambda e: e.tensor_add(onesf[:], onesf[:], onesf[:]))
        P.op("V", lambda e: e.memset(onesf[:], 1.0))
        last = _stt(n_ch - 1)
        return last

    def emit_stats(src, mu_t, rho_t, eps):
        for qc in range(NQC):
            c0 = qc * 512
            for j in range(NDC):
                sb, sq = stb[j % 2], stq[j % 2]
                P.use_buf("V", f"st{j % 2}")
                P.op("V", lambda e, s=src[j], sb=sb, c0=c0: e.tensor_copy(sb[:], s[:, c0:c0 + 512]))
                nv = P.op("V", lambda e, sb=sb, sq=sq: e.tensor_mul(sq[:], sb[:], sb[:]), inc="V")
                P.wait("T", "V", nv)
                if j == 0:
                    P.use_bank("T", "psa")
                    P.use_bank("T", "psb")
                P.op("T", lambda e, sb=sb, j=j: e.matmul(psa[0:1, :], ones_c[:], sb[:],
                                                         start=(j == 0), stop=(j == 7)))
                nt = P.op("T", lambda e, sq=sq, j=j: e.matmul(psb[0:1, :], ones_c[:], sq[:],
                                                              start=(j == 0), stop=(j == 7)), inc="T")
                P.set_buf(f"st{j % 2}", "T", nt)
            P.wait("V", "T", P.n["T"])
            r0 = rowf[0:1, 0:512]
            r1 = rowf[0:1, 512:1024]
            r2 = rowf[0:1, 1024:1536]
            P.op("V", lambda e: e.tensor_scalar_mul(r0, psa[0:1, :], 1.0 / D))
            P.op("V", lambda e: e.tensor_scalar_mul(r1, psb[0:1, :], 1.0 / D))
            P.op("V", lambda e: e.tensor_mul(r2, r0, r0))
            P.op("V", lambda e: e.tensor_copy(mu_row[:], r0))
            nv = P.op("V", lambda e: e.tensor_sub(r1, r1, r2), inc="V")
            P.wait("A", "V", nv)
            ecol = epsc[0:1, 0:1] if eps == LN_EPS else epsc[0:1, 1:2]
            P.op("A", lambda e, ecol=ecol: e.activation(r2, r1, AF.Ln, bias=ecol))
            na = P.op("A", lambda e: e.activation(rho_row[:], r2, AF.Exp, scale=-0.5), inc="A")
            P.wait("T", "A", na)
            P.use_bank("T", "psa")
            P.op("T", lambda e: e.matmul(psa[:], ones_r[:], mu_row[:], start=True, stop=True), inc="T")
            P.use_bank("T", "psb")
            n2 = P.op("T", lambda e: e.matmul(psb[:], ones_r[:], rho_row[:], start=True, stop=True), inc="T")
            P.wait("V", "T", n2)
            P.op("V", lambda e, mu_t=mu_t, c0=c0: e.tensor_copy(mu_t[:, c0:c0 + 512], psa[:]))
            nv = P.op("V", lambda e, rho_t=rho_t, c0=c0: e.tensor_copy(rho_t[:, c0:c0 + 512], psb[:]), inc="V")
            P.set_bank("psa", "V", nv)
            P.set_bank("psb", "V", nv)

    def emit_ln_u(src, mu_t, rho_t, eps, wait_S=0):
        if wait_S:
            P.wait("V", "S", wait_S)
        emit_stats(src, mu_t, rho_t, eps)
        last = None
        for j in range(NDC):
            for qc in range(NQC):
                c0 = qc * 512
                P.op("V", lambda e, s=src[j], mu_t=mu_t, c0=c0: e.tensor_sub(
                    zt[:], s[:, c0:c0 + 512], mu_t[:, c0:c0 + 512]))
                last = P.op("V", lambda e, j=j, rho_t=rho_t, c0=c0: e.tensor_mul(
                    U[j][:, c0:c0 + 512], zt[:], rho_t[:, c0:c0 + 512]), inc="V")
        return last

    def emit_attention(u_ready_V):
        P.wait("T", "V", u_ready_V)
        qk_v = None
        for t in range(NDC):
            P.op("V", lambda e, t=t: e.tensor_scalar_mul(qT[t][:], U[t][:], rots[:, t:t + 1]))
            qk_v = P.op("V", lambda e, t=t: e.tensor_scalar_mul(kT[t][:], U[t][:], ents[:, t:t + 1]), inc="V")
        P.wait("T", "V", qk_v)
        sq_bufs = [sqs[:, 0:512], sqs[:, 512:1024]]
        for t in range(NDC):
            for w, src in [(0, qT[t]), (1, kT[t])]:
                for qc in range(NQC):
                    i = (t * 2 + w) * 4 + qc
                    sb = sq_bufs[i % 2]
                    P.use_buf("V", f"sqb{i % 2}")
                    nv = P.op("V", lambda e, src=src, qc=qc, sb=sb: e.tensor_mul(
                        sb, src[:, qc * 512:(qc + 1) * 512], src[:, qc * 512:(qc + 1) * 512]), inc="V")
                    P.wait("T", "V", nv)
                    P.use_bank("T", "psa")
                    nt = P.op("T", lambda e, sb=sb: e.matmul(psa[0:1, :], ones_c[:], sb,
                                                             start=True, stop=True), inc="T")
                    P.set_buf(f"sqb{i % 2}", "T", nt)
                    P.wait("V", "T", nt)
                    nv = P.op("V", lambda e, i=i: e.reduce_max(
                        rowf[0:1, i:i + 1], psa[0:1, :], axis=mybir.AxisListType.X), inc="V")
                    P.set_bank("psa", "V", nv)
        for i in range(16):
            P.op("V", lambda e, i=i: e.reduce_max(
                rowf[0:1, 64 + i:65 + i], rowf[0:1, i * 4:(i + 1) * 4], axis=mybir.AxisListType.X))
        for t in range(NDC):
            P.op("V", lambda e, t=t: e.tensor_add(
                rowf[0:1, 96 + t:97 + t], rowf[0:1, 64 + 2 * t:65 + 2 * t],
                rowf[0:1, 65 + 2 * t:66 + 2 * t]))
        for t in range(NDC):
            nv = P.op("V", lambda e, t=t: e.tensor_scalar_mul(
                cells_b[0:1, t:t + 1], rowf[0:1, 96 + t:97 + t], -0.0625),
                inc="V" if t == 7 else None)
        P.wait("T", "V", nv)
        P.use_bank("T", "psb")
        nt = None
        for t in range(NDC):
            nt = P.op("T", lambda e, t=t: e.matmul(psb[:, t:t + 1], ones_r[:],
                                                   cells_b[0:1, t:t + 1], start=True, stop=True),
                      inc="T" if t == 7 else None)
        P.wait("V", "T", nt)
        nv = P.op("V", lambda e: e.tensor_copy(negc[:], psb[:, 0:8]), inc="V")
        P.set_bank("psb", "V", nv)
        negc_v = P.n["V"]
        P.wait("A", "V", negc_v)
        exp_hist = []
        for h in range(NH):
            tile = h // 2
            odd = h % 2
            p0 = odd * 64
            vst = 64 if odd else 65
            P.use_bank("T", "psbf")
            nt = None
            for kt in range(16):
                nt = P.op("T", lambda e, tile=tile, p0=p0, kt=kt: e.transpose(
                    psbf[:, kt * 64:(kt + 1) * 64],
                    U[tile][p0:p0 + 64, kt * 128:(kt + 1) * 128], id16[p0:p0 + 64, p0:p0 + 64]),
                    inc="T" if kt == 15 else None)
            P.wait("V", "T", nt)
            nv = None
            for kt in range(16):
                nv = P.op("V", lambda e, kt=kt, vst=vst: e.tensor_copy(
                    Vp[:, kt * vst:kt * vst + 64], psbf[:, kt * 64:(kt + 1) * 64]),
                    inc="V" if kt == 15 else None)
                if not odd:
                    P.op("V", lambda e, kt=kt: e.memset(Vp[:, kt * 65 + 64:kt * 65 + 65], 1.0))
            P.set_bank("psbf", "V", nv)
            vp_ready = nv
            for qc in range(NQC):
                c0 = qc * 512
                for g in range(4):
                    if exp_hist:
                        P.wait("T", "A", exp_hist[-1])
                    nt = None
                    for k4 in range(4):
                        kt = g * 4 + k4
                        nt = P.op("T", lambda e, tile=tile, p0=p0, kt=kt, c0=c0, k4=k4: e.matmul(
                            ps4[:, k4 * 512:(k4 + 1) * 512],
                            kT[tile][p0:p0 + 64, kt * 128:(kt + 1) * 128],
                            qT[tile][p0:p0 + 64, c0:c0 + 512], start=True, stop=True),
                            inc="T" if k4 == 3 else None)
                    P.wait("A", "T", nt)
                    P.use_buf("A", "At")
                    na = P.op("A", lambda e, tile=tile: e.activation(
                        At[:], ps4[:], AF.Exp, bias=negc[:, tile:tile + 1], scale=0.125), inc="A")
                    exp_hist.append(na)
                    P.wait("T", "A", na)
                    if g == 0:
                        P.wait("T", "V", vp_ready)
                        P.use_bank("T", "pso")
                        if odd:
                            P.use_bank("T", "psa")
                    nto = None
                    for k4 in range(4):
                        kt = g * 4 + k4
                        if odd:
                            P.op("T", lambda e, kt=kt, k4=k4: e.matmul(
                                pso[64:128, :], Vp[:, kt * 64:(kt + 1) * 64],
                                At[:, k4 * 512:(k4 + 1) * 512],
                                start=(kt == 0), stop=(kt == 15), tile_position=(0, 64)))
                            nto = P.op("T", lambda e, kt=kt, k4=k4: e.matmul(
                                psa[0:1, :], ones_c[:], At[:, k4 * 512:(k4 + 1) * 512],
                                start=(kt == 0), stop=(kt == 15)), inc="T" if k4 == 3 else None)
                        else:
                            nto = P.op("T", lambda e, kt=kt, k4=k4: e.matmul(
                                pso[0:65, :], Vp[:, kt * 65:(kt + 1) * 65],
                                At[:, k4 * 512:(k4 + 1) * 512],
                                start=(kt == 0), stop=(kt == 15)), inc="T" if k4 == 3 else None)
                    P.set_buf("At", "T", nto)
                P.wait("A", "T", P.n["T"])
                if odd:
                    P.op("A", lambda e: e.activation(rowf[0:1, 0:512], psa[0:1, :], AF.Ln))
                else:
                    P.op("A", lambda e: e.activation(rowf[0:1, 0:512], pso[64:65, :], AF.Ln))
                na = P.op("A", lambda e: e.activation(rs_row[:], rowf[0:1, 0:512], AF.Exp,
                                                      scale=-1.0), inc="A")
                if odd:
                    P.set_bank("psa", "A", na)
                P.wait("T", "A", na)
                P.use_bank("T", "psb")
                nt = P.op("T", lambda e: e.matmul(psb[:], ones_r[:], rs_row[:],
                                                  start=True, stop=True), inc="T")
                P.wait("V", "T", nt)
                P.op("V", lambda e, p0=p0: e.tensor_copy(rsb[p0:p0 + 64, :], psb[p0:p0 + 64, :]))
                P.op("V", lambda e, p0=p0: e.tensor_mul(
                    zt[p0:p0 + 64, :], pso[p0:p0 + 64, :], rsb[p0:p0 + 64, :]))
                nv = P.op("V", lambda e, tile=tile, p0=p0, c0=c0: e.tensor_add(
                    X[tile][p0:p0 + 64, c0:c0 + 512], X[tile][p0:p0 + 64, c0:c0 + 512],
                    zt[p0:p0 + 64, :]), inc="V")
                P.set_bank("pso", "V", nv)
                P.set_bank("psb", "V", nv)

    def dump_tiles(nm, tiles, wait_sem, wait_val):
        if not DEBUG:
            return
        P.wait("S", wait_sem, wait_val)
        for i, t_ in enumerate(tiles):
            P.op("S", lambda e, t_=t_, i=i, nm=nm: e.dma_start(
                out=dbg[nm][i * 128:(i + 1) * 128, :], in_=t_[:]), inc="S", dma=True)
        P.wait("V", "S", P.n["S"])

    def dump_row(nm, src, wait_sem, wait_val):
        if not DEBUG:
            return
        P.wait("S", wait_sem, wait_val)
        P.op("S", lambda e, src=src, nm=nm: e.dma_start(out=dbg[nm], in_=src[:]),
             inc="S", dma=True)
        P.wait("V", "S", P.n["S"])

    lnv = emit_ln_u(X, mu_m, rho_m, LN_EPS, wait_S=n_loadS)
    dump_tiles("d_u1", U, "V", lnv)
    P.wait("T", "G", n_loadG)
    cwv = emit_router(lnv)
    dump_row("d_cw1", cwT, "V", cwv)
    P.wait("S", "V", lnv)
    sp = None
    for i in range(NDC):
        sp = P.op("S", lambda e, i=i: e.dma_start(
            out=xsp[i * 128:(i + 1) * 128, :], in_=X[i][:]), inc="S", dma=True)
    last_mm2 = emit_ffn(sp, cwv)
    state2["ffn_calls"] = 1
    dump_tiles("d_s1", S, "V", P.n["V"])
    xu1 = emit_epilogue2(last_mm2)
    dump_tiles("d_x1", X, "V", xu1)
    lnv = emit_ln_u(X, mu_m, rho_m, LN_EPS)
    dump_tiles("d_u2", U, "V", lnv)
    emit_attention(lnv)
    dump_tiles("d_x2", X, "V", P.n["V"])
    lnv = emit_ln_u(X, mu_m, rho_m, LN_EPS)
    dump_tiles("d_u3", U, "V", lnv)
    cwv = emit_router(lnv)
    dump_row("d_cw2", cwT, "V", cwv)
    P.wait("S", "V", lnv)
    sp = None
    for i in range(NDC):
        sp = P.op("S", lambda e, i=i: e.dma_start(
            out=xsp[i * 128:(i + 1) * 128, :], in_=X[i][:]), inc="S", dma=True)
    last_mm2 = emit_ffn(sp, cwv)
    xupd = emit_epilogue2(last_mm2)
    P.wait("S", "V", xupd)
    for i in range(NDC):
        P.op("S", lambda e, i=i: e.dma_start(
            out=yt[i * 128:(i + 1) * 128, :], in_=X[i][:]), inc="S", dma=True)
    P.wait("S", "S", P.n["S"])

    with nc.Block() as block:
        @block.tensor
        def _(t):
            for f in P.ops["T"]:
                f(t)

        @block.scalar
        def _(a):
            for f in P.ops["A"]:
                f(a)

        @block.vector
        def _(v):
            for f in P.ops["V"]:
                f(v)

        @block.gpsimd
        def _(g):
            for f in P.ops["G"]:
                f(g)

        @block.sync
        def _(s):
            for f in P.ops["S"]:
                f(s)

    stack.close()
    return nc


_NC_CACHE = None


def kernel(L0_attn_inputs, L0_attn_rotation_params, L0_attn_entangle_params,
           L0_ffn_router_w, L0_ffn_router_b, L0_ffn_w1, L0_ffn_b1,
           L0_ffn_w2, L0_ffn_b2, L0_ffn_ln_w, L0_ffn_ln_b):
    global _NC_CACHE
    x = np.asarray(L0_attn_inputs, dtype=np.float32)
    rot = np.asarray(L0_attn_rotation_params, dtype=np.float32)
    ent = np.asarray(L0_attn_entangle_params, dtype=np.float32)
    wr = np.asarray(L0_ffn_router_w, dtype=np.float32)
    rb = np.asarray(L0_ffn_router_b, dtype=np.float32)
    w1 = np.asarray(L0_ffn_w1, dtype=np.float32)
    b1 = np.asarray(L0_ffn_b1, dtype=np.float32)
    w2 = np.asarray(L0_ffn_w2, dtype=np.float32)
    b2 = np.asarray(L0_ffn_b2, dtype=np.float32)
    lnw_a = np.asarray(L0_ffn_ln_w, dtype=np.float32)
    lnb_a = np.asarray(L0_ffn_ln_b, dtype=np.float32)

    w1_jit = np.ascontiguousarray(
        w1.reshape(NE, NFC, 128, NDC, 128).transpose(0, 1, 4, 3, 2)
    ).reshape(NE, NFC, 128, D)
    w2_jit = np.ascontiguousarray(
        w2.reshape(NE, NDC, 128, 2, 16, 128).transpose(0, 3, 1, 5, 4, 2)
    ).reshape(NE, 2, NDC, 128, 2048)
    b1s = np.ascontiguousarray(b1.reshape(NE, NFC, 128).transpose(2, 0, 1)).reshape(128, NE * NFC)
    b2s = np.ascontiguousarray(b2.reshape(NE, NDC, 128).transpose(2, 0, 1)).reshape(128, NE * NDC)
    wrts = np.ascontiguousarray(wr.T.reshape(NDC, 128, NE).transpose(1, 0, 2)).reshape(128, NDC * NE)
    lnw_s = np.ascontiguousarray(lnw_a.reshape(NDC, 128).T)
    lnb_s = np.ascontiguousarray(lnb_a.reshape(NDC, 128).T)
    rots_s = np.ascontiguousarray(rot.reshape(D).reshape(NDC, 128).T)
    ents_s = np.ascontiguousarray(ent.reshape(D).reshape(NDC, 128).T)
    rb_s = np.ascontiguousarray(rb.reshape(NE, 1))
    id16 = np.eye(128, dtype=ml_dtypes.bfloat16)
    id32 = np.eye(128, dtype=np.float32)
    ones_r = np.ones((1, 128), dtype=ml_dtypes.bfloat16)
    ones_c = np.ones((128, 1), dtype=ml_dtypes.bfloat16)

    if _NC_CACHE is None:
        _NC_CACHE = build_program()
    nc = _NC_CACHE

    common = {"w1j": w1_jit, "w2j": w2_jit, "b1s": b1s, "b2s": b2s,
              "wrts": wrts, "rb": rb_s, "lnw": lnw_s, "lnb": lnb_s,
              "rots": rots_s, "ents": ents_s, "id16": id16, "id32": id32,
              "ones_r": ones_r, "ones_c": ones_c,
              "epsc": np.array([[LN_EPS, FFN_LN_EPS]], dtype=np.float32)}
    in_maps = []
    for b in range(8):
        m = dict(common)
        m["xt"] = np.ascontiguousarray(x[b].T)
        in_maps.append(m)
    res = run_bass_kernel_spmd(nc, in_maps, core_ids=list(range(8)))
    out = np.empty((8, TOK, D), dtype=np.float32)
    for b in range(8):
        out[b] = res.results[b]["yt"].T
    if DEBUG:
        global DEBUG_OUT
        DEBUG_OUT = res.results
    return out


# version 05
# speedup vs baseline: 1.1661x; 1.1304x over previous
import os
import numpy as np
import ml_dtypes
import concourse.bass as bass
import concourse.mybir as mybir
from concourse.bass_utils import run_bass_kernel_spmd

F32 = mybir.dt.float32
BF16 = mybir.dt.bfloat16
AF = mybir.ActivationFunctionType
ALU = mybir.AluOpType

TOK = 2048
D = 1024
FF = 4096
NE = 8
NDC = D // 128
NQC = TOK // 512
NFC = FF // 128
NH = 16
HD = 64
LN_EPS = 1e-5
FFN_LN_EPS = 1e-12


class Plan:
    def __init__(self, sems):
        self.sems = sems
        self.n = {k: 0 for k in sems}
        self.ops = {k: [] for k in ["T", "A", "V", "G", "S"]}
        self.bank_user = {}
        self.buf_user = {}

    def wait(self, eng, sem, val):
        if val <= 0:
            return
        s = self.sems[sem]
        self.ops[eng].append(lambda e, s=s, v=val: e.wait_ge(s, v))

    def op(self, eng, fn, inc=None, dma=False):
        if inc is None:
            self.ops[eng].append(fn)
            return None
        amt = 16 if dma else 1
        self.n[inc] += amt
        tgt = self.n[inc]
        s = self.sems[inc]
        self.ops[eng].append(lambda e, f=fn, s=s, a=amt: f(e).then_inc(s, a))
        return tgt

    def use_bank(self, eng, tag):
        if tag in self.bank_user:
            sem, cnt = self.bank_user[tag]
            self.wait(eng, sem, cnt)

    def set_bank(self, tag, sem, cnt):
        self.bank_user[tag] = (sem, cnt)

    def use_buf(self, eng, tag):
        if tag in self.buf_user:
            sem, cnt = self.buf_user[tag]
            self.wait(eng, sem, cnt)

    def set_buf(self, tag, sem, cnt):
        self.buf_user[tag] = (sem, cnt)


DEBUG = bool(int(os.environ.get("KDBG", "0")))


def build_program():
    nc = bass.Bass("TRN2", target_bir_lowering=False)

    xt = nc.dram_tensor("xt", [D, TOK], F32, kind="ExternalInput").ap()
    w1j = nc.dram_tensor("w1j", [NE, NFC, 128, D], F32, kind="ExternalInput").ap()
    w2j = nc.dram_tensor("w2j", [NE, 2, NDC, 128, 2048], F32, kind="ExternalInput").ap()
    b1s_d = nc.dram_tensor("b1s", [128, NE * NFC], F32, kind="ExternalInput").ap()
    b2s_d = nc.dram_tensor("b2s", [128, NE * NDC], F32, kind="ExternalInput").ap()
    wrts_d = nc.dram_tensor("wrts", [128, NDC * NE], F32, kind="ExternalInput").ap()
    rb_d = nc.dram_tensor("rb", [NE, 1], F32, kind="ExternalInput").ap()
    lnw_d = nc.dram_tensor("lnw", [128, NDC], F32, kind="ExternalInput").ap()
    lnb_d = nc.dram_tensor("lnb", [128, NDC], F32, kind="ExternalInput").ap()
    rots_d = nc.dram_tensor("rots", [128, NDC], F32, kind="ExternalInput").ap()
    ents_d = nc.dram_tensor("ents", [128, NDC], F32, kind="ExternalInput").ap()
    id16_d = nc.dram_tensor("id16", [128, 128], BF16, kind="ExternalInput").ap()
    id32_d = nc.dram_tensor("id32", [128, 128], F32, kind="ExternalInput").ap()
    ones_r_d = nc.dram_tensor("ones_r", [1, 128], BF16, kind="ExternalInput").ap()
    ones_c_d = nc.dram_tensor("ones_c", [128, 1], BF16, kind="ExternalInput").ap()
    epsc_d = nc.dram_tensor("epsc", [1, 2], F32, kind="ExternalInput").ap()
    yt = nc.dram_tensor("yt", [D, TOK], F32, kind="ExternalOutput").ap()
    xsp = nc.dram_tensor("xsp", [D, TOK], F32, kind="Internal").ap()
    cwd = nc.dram_tensor("cwd", [NE, TOK], BF16, kind="Internal").ap()
    dbg = {}
    if DEBUG:
        for nm, shp, dt in [("d_u1", [D, TOK], BF16), ("d_cw1", [NE, TOK], BF16),
                            ("d_s1", [D, TOK], F32), ("d_x1", [D, TOK], F32),
                            ("d_u2", [D, TOK], BF16), ("d_x2", [D, TOK], F32),
                            ("d_u3", [D, TOK], BF16), ("d_cw2", [NE, TOK], BF16),
                            ("d_lgtok", [128, 128], F32), ("d_cwtok", [128, 128], F32),
                            ("d_eq1", [128, 8], F32), ("d_eq2", [128, 8], F32),
                            ("d_tq1", [128, 8], F32), ("d_wc", [128, 4], F32),
                            ("d_ms", [128, 8], F32), ("d_lgT", [NE, TOK], F32),
                            ("d_cwb", [128, TOK], BF16), ("d_cwrow", [1, TOK], BF16),
                            ("d_s0", [D, TOK], F32), ("d_s1e", [D, TOK], F32),
                            ("d_cwbs", [NE, 128, TOK], BF16),
                            ("d_h0", [128, TOK], BF16), ("d_h15", [128, TOK], BF16),
                            ("d_ps2", [128, 512], F32), ("d_zt", [128, 512], F32),
                            ("d_u1b", [D, TOK], BF16), ("d_him", [128, 512], BF16)]:
            dbg[nm] = nc.dram_tensor(nm, shp, dt, kind="ExternalOutput").ap()

    base = [16896]

    def alloc(name, shape, dtype):
        nbytes = shape[-1] * (2 if dtype == BF16 else 4)
        nbytes = (nbytes + 31) & ~31
        t = nc.alloc_sbuf_tensor_at(name, shape, dtype, offset=base[0])
        base[0] += nbytes
        return t

    def alloc_at(name, shape, dtype, off):
        return nc.alloc_sbuf_tensor_at(name, shape, dtype, offset=off)

    R1 = base[0]
    X = [alloc_at(f"X{i}", [128, TOK], F32, R1 + i * 8192) for i in range(NDC)]
    H = [alloc_at(f"H{i}", [128, TOK], BF16, R1 + i * 4096) for i in range(16)]
    base[0] += 65536
    R2 = base[0]
    S = [alloc_at(f"S{i}", [128, TOK], F32, R2 + i * 8192) for i in range(NDC)]
    qT = [alloc_at(f"qT{i}", [128, TOK], BF16, R2 + i * 4096) for i in range(NDC)]
    kT = [alloc_at(f"kT{i}", [128, TOK], BF16, R2 + 32768 + i * 4096) for i in range(NDC)]
    mu_m = alloc_at("mu_m", [128, TOK], BF16, R2)
    rho_m = alloc_at("rho_m", [128, TOK], BF16, R2 + 4096)
    base[0] += 65536
    U = [alloc(f"U{i}", [128, TOK], BF16) for i in range(NDC)]
    W1s = [alloc(f"W1s{i}", [128, 1024], BF16) for i in range(2)]
    W2s = [alloc(f"W2s{i}", [128, 2048], BF16) for i in range(2)]
    PS = base[0]
    lgT = alloc_at("lgT", [NE, TOK], F32, PS)
    mu_e = alloc_at("mu_e", [128, TOK], BF16, PS)
    rho_e = alloc_at("rho_e", [128, TOK], BF16, PS + 4096)
    Vp = alloc_at("Vp", [128, 16 * 65], BF16, PS)
    At = alloc_at("At", [128, 2048], BF16, PS + 2176)
    rsb = alloc_at("rsb", [128, 512], BF16, PS + 6272)
    sqs = alloc_at("sqs", [128, 1024], BF16, PS + 2176)
    cwrow = alloc_at("cwrow", [1, TOK], BF16, PS + 4096)
    base[0] += 8192
    cwb = alloc("cwb", [128, TOK], BF16)
    At2 = alloc_at("At2", [128, 2048], BF16, base[0] - 4096)
    cwT = alloc("cwT", [NE, TOK], BF16)
    lgtok = alloc("lgtok", [128, 128], F32)
    msA = alloc("msA", [128, 128], F32)
    wc16 = alloc("wc16", [128, 48], F32)
    onesf = alloc("onesf", [128, 16], F32)
    zt = alloc("zt", [128, 512], F32)
    stb = [alloc(f"stb{i}", [128, 512], BF16) for i in range(2)]
    stq = [alloc(f"stq{i}", [128, 512], BF16) for i in range(2)]
    rowf = alloc("rowf", [1, 1536], F32)
    mu_row = alloc("mu_row", [1, 512], BF16)
    rho_row = alloc("rho_row", [1, 512], BF16)
    rs_row = alloc("rs_row", [1, 512], BF16)
    cells_b = alloc("cells_b", [1, 32], BF16)
    negc = alloc("negc", [128, 8], F32)
    b1s = alloc("b1s", [128, NE * NFC], F32)
    b2s = alloc("b2s", [128, NE * NDC], F32)
    wrts = alloc("wrts", [128, NDC * NE], BF16)
    rbc = alloc("rbc", [NE, 1], F32)
    lnw = alloc("lnw", [128, NDC], F32)
    lnb = alloc("lnb", [128, NDC], F32)
    rots = alloc("rots", [128, NDC], F32)
    ents = alloc("ents", [128, NDC], F32)
    id16 = alloc("id16", [128, 128], BF16)
    id32 = alloc("id32", [128, 128], F32)
    ones_r = alloc("ones_r", [1, 128], BF16)
    ones_c = alloc("ones_c", [128, 1], BF16)
    epsc = alloc("epsc", [1, 2], F32)
    nc.sbuf_base = base[0]
    assert base[0] <= 229000, f"sbuf overflow: {base[0]}"

    ps4 = nc.alloc_psum_tensor("ps4", [128, 2048], F32)
    ps1 = [ps4[:, 0:512], ps4[:, 512:1024]]
    ps2 = [ps4[:, 1024:1536], ps4[:, 1536:2048]]
    psa = nc.alloc_psum_tensor("psa", [128, 512], F32)
    psb = nc.alloc_psum_tensor("psb", [128, 512], F32)
    pso = nc.alloc_psum_tensor("pso", [128, 512], F32)
    psbf = nc.alloc_psum_tensor("psbf", [128, 1024], BF16)

    import contextlib
    stack = contextlib.ExitStack()
    semT = stack.enter_context(nc.semaphore("semT"))
    semA = stack.enter_context(nc.semaphore("semA"))
    semV = stack.enter_context(nc.semaphore("semV"))
    semG = stack.enter_context(nc.semaphore("semG"))
    semS = stack.enter_context(nc.semaphore("semS"))
    P = Plan({"T": semT, "A": semA, "V": semV, "G": semG, "S": semS})
    for _sm in (semT, semA, semV, semG, semS):
        nc.gpsimd.sem_clear(_sm)
    nc.all_engine_barrier()

    for i in range(NDC):
        P.op("S", lambda e, i=i: e.dma_start(out=X[i][:], in_=xt[i * 128:(i + 1) * 128, :]),
             inc="S", dma=True)
    for dst, src in [(b1s, b1s_d), (b2s, b2s_d), (rbc, rb_d), (lnw, lnw_d),
                     (lnb, lnb_d), (rots, rots_d), (ents, ents_d),
                     (id16, id16_d), (id32, id32_d), (ones_r, ones_r_d),
                     (ones_c, ones_c_d), (epsc, epsc_d)]:
        P.op("S", lambda e, d=dst, s=src: e.dma_start(out=d[:], in_=s), inc="S", dma=True)
    P.op("G", lambda e: e.dma_start(out=wrts[:], in_=wrts_d), inc="G", dma=True)
    n_loadS = P.n["S"]
    n_loadG = P.n["G"]

    state2 = {"last_cwrow_S": 0}

    def emit_router(u_ready_V):
        P.wait("T", "V", u_ready_V)
        for qc in range(NQC):
            c0 = qc * 512
            P.use_bank("T", "psa")
            nt = None
            for j in range(NDC):
                nt = P.op("T", lambda e, j=j, c0=c0: e.matmul(
                    psa[0:NE, :], wrts[:, j * NE:(j + 1) * NE], U[j][:, c0:c0 + 512],
                    start=(j == 0), stop=(j == 7)), inc="T" if j == 7 else None)
            P.wait("V", "T", nt)
            nv = P.op("V", lambda e, c0=c0: e.tensor_scalar_add(
                lgT[:, c0:c0 + 512], psa[0:NE, :], rbc[:]), inc="V")
            P.set_bank("psa", "V", nv)
        P.wait("T", "V", P.n["V"])
        P.use_bank("T", "psb")
        nt = None
        for t in range(16):
            nt = P.op("T", lambda e, t=t: e.transpose(
                psb[:, t * 8:(t + 1) * 8], lgT[:, t * 128:(t + 1) * 128], id32[0:NE, 0:NE]),
                inc="T" if t == 15 else None)
        P.wait("V", "T", nt)
        nv = P.op("V", lambda e: e.tensor_copy(lgtok[:], psb[:, 0:128]), inc="V")
        P.set_bank("psb", "V", nv)
        m1b = zt[:, 0:128]
        m2b = zt[:, 128:256]
        qA1 = zt[:, 256:384]
        qA2 = zt[:, 384:512]
        cwtok = zt[:, 0:128]
        P.op("V", lambda e: e.memset(onesf[:], 1.0))
        P.op("V", lambda e: e.memset(wc16[:], 0.0))
        for t in range(16):
            P.op("V", lambda e, t=t: e.max(msA[:, t * 8:(t + 1) * 8], lgtok[:, t * 8:(t + 1) * 8]))
        for t in range(16):
            P.op("V", lambda e, t=t: e.tensor_sub(
                wc16[:, t:t + 1], msA[:, t * 8 + 1:t * 8 + 2], msA[:, t * 8:t * 8 + 1]))
        nv = P.op("V", lambda e: e.tensor_copy(msA[:], msA[:]), inc="V")
        P.wait("A", "V", nv)
        na = P.op("A", lambda e: e.activation(wc16[:, 16:32], wc16[:, 0:16], AF.Sigmoid,
                                              scale=-1.0), inc="A")
        P.wait("V", "A", na)
        P.op("V", lambda e: e.tensor_sub(wc16[:, 32:48], onesf[:], wc16[:, 16:32]))
        for t in range(16):
            P.op("V", lambda e, t=t: e.tensor_scalar_mul(
                m1b[:, t * 8:(t + 1) * 8], onesf[:, 0:8], msA[:, t * 8:t * 8 + 1]))
        for t in range(16):
            P.op("V", lambda e, t=t: e.tensor_scalar_mul(
                m2b[:, t * 8:(t + 1) * 8], onesf[:, 0:8], msA[:, t * 8 + 1:t * 8 + 2]))
        P.op("V", lambda e: e.tensor_tensor(qA1[:], lgtok[:], m1b[:], ALU.is_equal))
        P.op("V", lambda e: e.tensor_tensor(qA2[:], lgtok[:], m2b[:], ALU.is_equal))
        P.op("V", lambda e: e.tensor_mul(m1b[:], qA1[:], qA2[:]))
        P.op("V", lambda e: e.tensor_sub(qA2[:], qA2[:], m1b[:]))
        for t in range(16):
            P.op("V", lambda e, t=t: e.tensor_scalar_mul(
                qA1[:, t * 8:(t + 1) * 8], qA1[:, t * 8:(t + 1) * 8], wc16[:, 16 + t:17 + t]))
        for t in range(16):
            P.op("V", lambda e, t=t: e.tensor_scalar_mul(
                qA2[:, t * 8:(t + 1) * 8], qA2[:, t * 8:(t + 1) * 8], wc16[:, 32 + t:33 + t]))
        P.op("V", lambda e: e.tensor_add(cwtok, qA1[:], qA2[:]), inc="V")
        P.wait("T", "V", P.n["V"])
        for g in range(4):
            P.use_bank("T", "psa")
            nt = None
            for k in range(4):
                t = g * 4 + k
                nt = P.op("T", lambda e, t=t, k=k: e.transpose(
                    psa[0:8, k * 128:(k + 1) * 128], zt[:, t * 8:t * 8 + 8],
                    id32[:, 0:128]), inc="T" if k == 3 else None)
            P.wait("V", "T", nt)
            if g == 0:
                P.wait("V", "S", state2.get("cw_dram_S", 0))
            nv = P.op("V", lambda e, g=g: e.tensor_copy(cwT[:, g * 512:(g + 1) * 512], psa[0:8, :]), inc="V")
            P.set_bank("psa", "V", nv)
        P.wait("S", "V", P.n["V"])
        ns = P.op("S", lambda e: e.dma_start(out=cwd, in_=cwT[:]), inc="S", dma=True)
        state2["cw_dram_S"] = ns
        return P.n["V"]

    def emit_ffn(spill_cnt, cw_ready_V):
        gelu_hist = []
        comb_hist = []
        prev_mm2_T = 0
        cwrow_T = 0
        cwb_evac_V = 0
        first_gelu = True
        for e_i in range(NE):
            P.wait("S", "T", cwrow_T)
            if e_i == 0:
                P.wait("S", "V", cw_ready_V)
            ns = P.op("S", lambda e, e_i=e_i: e.dma_start(
                out=cwrow[:], in_=cwd[e_i:e_i + 1, :]), inc="S", dma=True)
            state2["last_cwrow_S"] = ns
            P.wait("T", "S", ns)
            P.wait("T", "V", cwb_evac_V)
            for qc in range(NQC):
                bank = [psa, psb][qc % 2]
                tag = "psa" if qc % 2 == 0 else "psb"
                P.use_bank("T", tag)
                nt = P.op("T", lambda e, bank=bank, qc=qc: e.matmul(
                    bank[:], ones_r[:], cwrow[0:1, qc * 512:(qc + 1) * 512],
                    start=True, stop=True), inc="T")
                if qc == 3:
                    cwrow_T = nt
                P.wait("V", "T", nt)
                nv = P.op("V", lambda e, bank=bank, qc=qc: e.tensor_copy(
                    cwb[:, qc * 512:(qc + 1) * 512], bank[:]), inc="V")
                P.set_bank(tag, "V", nv)
                if qc == 3:
                    cwb_evac_V = nv
            if DEBUG and e_i == 1 and not state2.get("dumped_u1b") and state2.get("ffn_calls", 0) == 0:
                state2["dumped_u1b"] = True
                P.wait("S", "V", P.n["V"])
                for i_ in range(NDC):
                    P.op("S", lambda e, i_=i_: e.dma_start(
                        out=dbg["d_u1b"][i_ * 128:(i_ + 1) * 128, :], in_=U[i_][:]), inc="S", dma=True)
                P.wait("T", "S", P.n["S"])
            if DEBUG and not state2.get("dumped_cwbs_" + str(e_i)) and state2.get("ffn_calls", 0) == 0:
                state2["dumped_cwbs_" + str(e_i)] = True
                P.wait("S", "V", cwb_evac_V)
                P.op("S", lambda e, e_i=e_i: e.dma_start(out=dbg["d_cwbs"][e_i], in_=cwb[:]), inc="S", dma=True)
                P.wait("V", "S", P.n["S"])
            for fh in range(2):
                gelu_this_fh = None
                for fcl in range(16):
                    slot = fcl % 2
                    P.use_buf("G", ("w1", slot))
                    fcg = fh * 16 + fcl
                    ng = P.op("G", lambda e, e_i=e_i, fcg=fcg, slot=slot: e.dma_start(
                        out=W1s[slot][:], in_=w1j[e_i, fcg]), inc="G", dma=True)
                    P.wait("T", "G", ng)
                    for qc in range(NQC):
                        ci = fcl * 4 + qc
                        bank = ps1[ci % 2]
                        if len(gelu_hist) >= 2:
                            P.wait("T", "A", gelu_hist[-2])
                        nt = None
                        for j in range(NDC):
                            nt = P.op("T", lambda e, slot=slot, j=j, qc=qc, bank=bank: e.matmul(
                                bank, W1s[slot][:, j * 128:(j + 1) * 128],
                                U[j][:, qc * 512:(qc + 1) * 512],
                                start=(j == 0), stop=(j == 7)), inc="T" if j == 7 else None)
                        if qc == 3:
                            P.set_buf(("w1", slot), "T", nt)
                        if DEBUG and e_i == 1 and fh == 0 and fcl == 0 and qc == 0 and state2.get("ffn_calls", 0) == 0 and not state2.get("dumped_ps1"):
                            state2["dumped_ps1"] = True
                            P.wait("V", "T", nt)
                            nvd = P.op("V", lambda e, bank=bank: e.tensor_copy(zt[:], bank), inc="V")
                            P.wait("S", "V", nvd)
                            P.op("S", lambda e: e.dma_start(out=dbg["d_zt"], in_=zt[:]), inc="S", dma=True)
                            P.wait("A", "S", P.n["S"])
                        P.wait("A", "T", nt)
                        if first_gelu:
                            P.wait("A", "S", spill_cnt)
                            first_gelu = False
                        if fcl == 0 and qc == 0 and prev_mm2_T:
                            P.wait("A", "T", prev_mm2_T)
                        bcol = e_i * NFC + fh * 16 + fcl
                        na = P.op("A", lambda e, bank=bank, fcl=fcl, qc=qc, bcol=bcol: e.activation(
                            H[fcl][:, qc * 512:(qc + 1) * 512], bank, AF.Gelu,
                            bias=b1s[:, bcol:bcol + 1], scale=1.0), inc="A")
                        gelu_hist.append(na)
                        gelu_this_fh = na
                        if DEBUG and e_i == 1 and fh == 0 and fcl == 0 and qc == 3 and state2.get("ffn_calls", 0) == 0 and not state2.get("dumped_himm"):
                            state2["dumped_himm"] = True
                            P.wait("S", "A", na)
                            P.op("S", lambda e: e.dma_start(out=dbg["d_him"], in_=H[0][:, 0:512]), inc="S", dma=True)
                            P.wait("A", "S", P.n["S"])
                for dc in range(NDC):
                    slot = dc % 2
                    P.use_buf("G", ("w2", slot))
                    ng = P.op("G", lambda e, e_i=e_i, fh=fh, dc=dc, slot=slot: e.dma_start(
                        out=W2s[slot][:], in_=w2j[e_i, fh, dc]), inc="G", dma=True)
                    P.wait("T", "G", ng)
                    for qc in range(NQC):
                        ci = dc * 4 + qc
                        bank = ps2[ci % 2]
                        if dc == 0 and qc == 0:
                            P.wait("T", "A", gelu_this_fh)
                        if len(comb_hist) >= 2:
                            P.wait("T", "V", comb_hist[-2])
                        nt = None
                        for fcl in range(16):
                            nt = P.op("T", lambda e, slot=slot, fcl=fcl, dc=dc, qc=qc, bank=bank: e.matmul(
                                bank, W2s[slot][:, fcl * 128:(fcl + 1) * 128],
                                H[fcl][:, qc * 512:(qc + 1) * 512],
                                start=(fcl == 0), stop=(fcl == 15)), inc="T" if fcl == 15 else None)
                        if qc == 3:
                            P.set_buf(("w2", slot), "T", nt)
                            if dc == 7:
                                prev_mm2_T = nt
                        P.wait("V", "T", nt)
                        c0 = qc * 512
                        if DEBUG and e_i == 1 and fh == 0 and dc == 0 and qc == 0 and state2.get("ffn_calls", 0) == 0:
                            nvd = P.op("V", lambda e, bank=bank: e.tensor_copy(zt[:], bank), inc="V")
                            P.wait("S", "V", nvd)
                            P.op("S", lambda e: e.dma_start(out=dbg["d_ps2"], in_=zt[:]), inc="S", dma=True)
                            P.op("S", lambda e: e.dma_start(out=dbg["d_h0"], in_=H[0][:]), inc="S", dma=True)
                            P.op("S", lambda e: e.dma_start(out=dbg["d_h15"], in_=H[15][:]), inc="S", dma=True)
                            P.wait("V", "S", P.n["S"])
                        if fh == 0:
                            b2col = e_i * NDC + dc
                            P.op("V", lambda e, bank=bank, b2col=b2col: e.tensor_scalar_add(
                                zt[:], bank, b2s[:, b2col:b2col + 1]))
                            nv1 = P.op("V", lambda e, c0=c0: e.tensor_mul(
                                zt[:], zt[:], cwb[:, c0:c0 + 512]), inc="V")
                        else:
                            nv1 = P.op("V", lambda e, bank=bank, c0=c0: e.tensor_mul(
                                zt[:], bank, cwb[:, c0:c0 + 512]), inc="V")
                        comb_hist.append(nv1)
                        if e_i == 0 and fh == 0:
                            nv2 = P.op("V", lambda e, dc=dc, c0=c0: e.tensor_copy(
                                S[dc][:, c0:c0 + 512], zt[:]), inc="V")
                        else:
                            nv2 = P.op("V", lambda e, dc=dc, c0=c0: e.tensor_add(
                                S[dc][:, c0:c0 + 512], S[dc][:, c0:c0 + 512], zt[:]), inc="V")
            if DEBUG and e_i == 1 and not state2.get("dumped_s1e"):
                state2["dumped_s1e"] = True
                P.wait("S", "V", nv2)
                for i_ in range(NDC):
                    P.op("S", lambda e, i_=i_: e.dma_start(
                        out=dbg["d_s1e"][i_ * 128:(i_ + 1) * 128, :], in_=S[i_][:]), inc="S", dma=True)
                P.wait("V", "S", P.n["S"])
            if DEBUG and e_i == 0 and not state2.get("dumped_s0"):
                state2["dumped_s0"] = True
                P.wait("S", "V", nv2)
                for i_ in range(NDC):
                    P.op("S", lambda e, i_=i_: e.dma_start(
                        out=dbg["d_s0"][i_ * 128:(i_ + 1) * 128, :], in_=S[i_][:]), inc="S", dma=True)
                P.wait("V", "S", P.n["S"])
        return prev_mm2_T

    def emit_epilogue2(last_mm2_T):
        P.wait("S", "T", last_mm2_T)
        rel = None
        for i in range(NDC):
            rel = P.op("S", lambda e, i=i: e.dma_start(
                out=X[i][:], in_=xsp[i * 128:(i + 1) * 128, :]), inc="S", dma=True)
        for j in range(NDC):
            for qc in range(NQC):
                c0 = qc * 512
                P.op("V", lambda e, j=j, c0=c0: e.tensor_add(
                    S[j][:, c0:c0 + 512], S[j][:, c0:c0 + 512], U[j][:, c0:c0 + 512]), inc="V")
        emit_stats(S, mu_e, rho_e, FFN_LN_EPS)
        P.wait("V", "S", P.n["S"])
        chunks = [(j, c) for j in range(NDC) for c in range(16)]
        n_ch = len(chunks)

        def _sub(i):
            j, c = chunks[i]
            b = zt[:, (i % 4) * 128:(i % 4) * 128 + 128]
            P.op("V", lambda e, j=j, c=c, b=b: e.tensor_sub(
                b, S[j][:, c * 128:(c + 1) * 128], mu_e[:, c * 128:(c + 1) * 128]))

        def _mul(i):
            j, c = chunks[i]
            b = zt[:, (i % 4) * 128:(i % 4) * 128 + 128]
            P.op("V", lambda e, c=c, b=b: e.tensor_mul(b, b, rho_e[:, c * 128:(c + 1) * 128]))

        def _ts(i):
            j, c = chunks[i]
            b = zt[:, (i % 4) * 128:(i % 4) * 128 + 128]
            P.op("V", lambda e, j=j, b=b: e.tensor_scalar(
                b, b, lnw[:, j:j + 1], lnb[:, j:j + 1], ALU.mult, ALU.add))

        def _stt(i):
            j, c = chunks[i]
            b = zt[:, (i % 4) * 128:(i % 4) * 128 + 128]
            return P.op("V", lambda e, j=j, c=c, b=b: e.scalar_tensor_tensor(
                X[j][:, c * 128:(c + 1) * 128], b, 0.5, X[j][:, c * 128:(c + 1) * 128],
                ALU.mult, ALU.add), inc="V")

        last = None
        xupd_j = {}
        for i in range(n_ch):
            _sub(i)
            _mul(i)
            if i >= 1:
                _ts(i - 1)
            if i >= 2:
                last = _stt(i - 2)
                xupd_j[chunks[i - 2][0]] = last
        _ts(n_ch - 1)
        last = _stt(n_ch - 2)
        xupd_j[chunks[n_ch - 2][0]] = last
        P.op("V", lambda e: e.tensor_add(onesf[:], onesf[:], onesf[:]))
        P.op("V", lambda e: e.memset(onesf[:], 1.0))
        last = _stt(n_ch - 1)
        xupd_j[chunks[n_ch - 1][0]] = last
        state2["xupd_j"] = xupd_j
        return last

    def emit_stats(src, mu_t, rho_t, eps):
        for qc in range(NQC):
            c0 = qc * 512
            for j in range(NDC):
                sb, sq = stb[j % 2], stq[j % 2]
                P.use_buf("V", f"st{j % 2}")
                P.op("V", lambda e, s=src[j], sb=sb, c0=c0: e.tensor_copy(sb[:], s[:, c0:c0 + 512]))
                nv = P.op("V", lambda e, sb=sb, sq=sq: e.tensor_mul(sq[:], sb[:], sb[:]), inc="V")
                P.wait("T", "V", nv)
                if j == 0:
                    P.use_bank("T", "psa")
                    P.use_bank("T", "psb")
                P.op("T", lambda e, sb=sb, j=j: e.matmul(psa[0:1, :], ones_c[:], sb[:],
                                                         start=(j == 0), stop=(j == 7)))
                nt = P.op("T", lambda e, sq=sq, j=j: e.matmul(psb[0:1, :], ones_c[:], sq[:],
                                                              start=(j == 0), stop=(j == 7)), inc="T")
                P.set_buf(f"st{j % 2}", "T", nt)
            P.wait("V", "T", P.n["T"])
            r0 = rowf[0:1, 0:512]
            r1 = rowf[0:1, 512:1024]
            r2 = rowf[0:1, 1024:1536]
            P.op("V", lambda e: e.tensor_scalar_mul(r0, psa[0:1, :], 1.0 / D))
            P.op("V", lambda e: e.tensor_scalar_mul(r1, psb[0:1, :], 1.0 / D))
            P.op("V", lambda e: e.tensor_mul(r2, r0, r0))
            P.op("V", lambda e: e.tensor_copy(mu_row[:], r0))
            nv = P.op("V", lambda e: e.tensor_sub(r1, r1, r2), inc="V")
            P.wait("A", "V", nv)
            ecol = epsc[0:1, 0:1] if eps == LN_EPS else epsc[0:1, 1:2]
            P.op("A", lambda e, ecol=ecol: e.activation(r2, r1, AF.Ln, bias=ecol))
            na = P.op("A", lambda e: e.activation(rho_row[:], r2, AF.Exp, scale=-0.5), inc="A")
            P.wait("T", "A", na)
            P.use_bank("T", "psa")
            P.op("T", lambda e: e.matmul(psa[:], ones_r[:], mu_row[:], start=True, stop=True), inc="T")
            P.use_bank("T", "psb")
            n2 = P.op("T", lambda e: e.matmul(psb[:], ones_r[:], rho_row[:], start=True, stop=True), inc="T")
            P.wait("V", "T", n2)
            P.op("V", lambda e, mu_t=mu_t, c0=c0: e.tensor_copy(mu_t[:, c0:c0 + 512], psa[:]))
            nv = P.op("V", lambda e, rho_t=rho_t, c0=c0: e.tensor_copy(rho_t[:, c0:c0 + 512], psb[:]), inc="V")
            P.set_bank("psa", "V", nv)
            P.set_bank("psb", "V", nv)

    def emit_ln_u(src, mu_t, rho_t, eps, wait_S=0):
        if wait_S:
            P.wait("V", "S", wait_S)
        emit_stats(src, mu_t, rho_t, eps)
        last = None
        for j in range(NDC):
            for qc in range(NQC):
                c0 = qc * 512
                P.op("V", lambda e, s=src[j], mu_t=mu_t, c0=c0: e.tensor_sub(
                    zt[:], s[:, c0:c0 + 512], mu_t[:, c0:c0 + 512]))
                last = P.op("V", lambda e, j=j, rho_t=rho_t, c0=c0: e.tensor_mul(
                    U[j][:, c0:c0 + 512], zt[:], rho_t[:, c0:c0 + 512]), inc="V")
        return last

    def emit_attention(u_ready_V):
        P.wait("T", "V", u_ready_V)
        qk_v = None
        for t in range(NDC):
            P.op("V", lambda e, t=t: e.tensor_scalar_mul(qT[t][:], U[t][:], rots[:, t:t + 1]))
            qk_v = P.op("V", lambda e, t=t: e.tensor_scalar_mul(kT[t][:], U[t][:], ents[:, t:t + 1]), inc="V")
        P.wait("T", "V", qk_v)
        sq_bufs = [sqs[:, 0:512], sqs[:, 512:1024]]
        for t in range(NDC):
            for w, src in [(0, qT[t]), (1, kT[t])]:
                for qc in range(NQC):
                    i = (t * 2 + w) * 4 + qc
                    sb = sq_bufs[i % 2]
                    P.use_buf("V", f"sqb{i % 2}")
                    nv = P.op("V", lambda e, src=src, qc=qc, sb=sb: e.tensor_mul(
                        sb, src[:, qc * 512:(qc + 1) * 512], src[:, qc * 512:(qc + 1) * 512]), inc="V")
                    P.wait("T", "V", nv)
                    P.use_bank("T", "psa")
                    nt = P.op("T", lambda e, sb=sb: e.matmul(psa[0:1, :], ones_c[:], sb,
                                                             start=True, stop=True), inc="T")
                    P.set_buf(f"sqb{i % 2}", "T", nt)
                    P.wait("V", "T", nt)
                    nv = P.op("V", lambda e, i=i: e.reduce_max(
                        rowf[0:1, i:i + 1], psa[0:1, :], axis=mybir.AxisListType.X), inc="V")
                    P.set_bank("psa", "V", nv)
        for i in range(16):
            P.op("V", lambda e, i=i: e.reduce_max(
                rowf[0:1, 64 + i:65 + i], rowf[0:1, i * 4:(i + 1) * 4], axis=mybir.AxisListType.X))
        for t in range(NDC):
            P.op("V", lambda e, t=t: e.tensor_add(
                rowf[0:1, 96 + t:97 + t], rowf[0:1, 64 + 2 * t:65 + 2 * t],
                rowf[0:1, 65 + 2 * t:66 + 2 * t]))
        for t in range(NDC):
            nv = P.op("V", lambda e, t=t: e.tensor_scalar_mul(
                cells_b[0:1, t:t + 1], rowf[0:1, 96 + t:97 + t], -0.0625),
                inc="V" if t == 7 else None)
        P.wait("T", "V", nv)
        P.use_bank("T", "psb")
        nt = None
        for t in range(NDC):
            nt = P.op("T", lambda e, t=t: e.matmul(psb[:, t:t + 1], ones_r[:],
                                                   cells_b[0:1, t:t + 1], start=True, stop=True),
                      inc="T" if t == 7 else None)
        P.wait("V", "T", nt)
        nv = P.op("V", lambda e: e.tensor_copy(negc[:], psb[:, 0:8]), inc="V")
        P.set_bank("psb", "V", nv)
        negc_v = P.n["V"]
        P.wait("A", "V", negc_v)
        exp_hist = []
        for h in range(NH):
            tile = h // 2
            odd = h % 2
            p0 = odd * 64
            vst = 64 if odd else 65
            P.use_bank("T", "psbf")
            nt = None
            for kt in range(16):
                nt = P.op("T", lambda e, tile=tile, p0=p0, kt=kt: e.transpose(
                    psbf[:, kt * 64:(kt + 1) * 64],
                    U[tile][p0:p0 + 64, kt * 128:(kt + 1) * 128], id16[p0:p0 + 64, p0:p0 + 64]),
                    inc="T" if kt == 15 else None)
            P.wait("V", "T", nt)
            nv = None
            for kt in range(16):
                nv = P.op("V", lambda e, kt=kt, vst=vst: e.tensor_copy(
                    Vp[:, kt * vst:kt * vst + 64], psbf[:, kt * 64:(kt + 1) * 64]),
                    inc="V" if kt == 15 else None)
                if not odd:
                    P.op("V", lambda e, kt=kt: e.memset(Vp[:, kt * 65 + 64:kt * 65 + 65], 1.0))
            P.set_bank("psbf", "V", nv)
            vp_ready = nv
            for qc in range(NQC):
                c0 = qc * 512
                Ats = [At, At2]

                def mm_s_group(g):
                    nt = None
                    for k4 in range(4):
                        kt = g * 4 + k4
                        nt = P.op("T", lambda e, tile=tile, p0=p0, kt=kt, c0=c0, k4=k4: e.matmul(
                            ps4[:, k4 * 512:(k4 + 1) * 512],
                            kT[tile][p0:p0 + 64, kt * 128:(kt + 1) * 128],
                            qT[tile][p0:p0 + 64, c0:c0 + 512], start=True, stop=True),
                            inc="T" if k4 == 3 else None)
                    return nt

                def exp_group(g, nt):
                    P.wait("A", "T", nt)
                    P.use_buf("A", f"At{g % 2}")
                    na = P.op("A", lambda e, tile=tile, g=g: e.activation(
                        Ats[g % 2][:], ps4[:], AF.Exp, bias=negc[:, tile:tile + 1],
                        scale=0.125), inc="A")
                    exp_hist.append(na)
                    return na

                def mm_o_group(g):
                    at = Ats[g % 2]
                    nto = None
                    for k4 in range(4):
                        kt = g * 4 + k4
                        if odd:
                            P.op("T", lambda e, kt=kt, k4=k4, at=at: e.matmul(
                                pso[64:128, :], Vp[:, kt * 64:(kt + 1) * 64],
                                at[:, k4 * 512:(k4 + 1) * 512],
                                start=(kt == 0), stop=(kt == 15), tile_position=(0, 64)))
                            nto = P.op("T", lambda e, kt=kt, k4=k4, at=at: e.matmul(
                                psa[0:1, :], ones_c[:], at[:, k4 * 512:(k4 + 1) * 512],
                                start=(kt == 0), stop=(kt == 15)), inc="T" if k4 == 3 else None)
                        else:
                            nto = P.op("T", lambda e, kt=kt, k4=k4, at=at: e.matmul(
                                pso[0:65, :], Vp[:, kt * 65:(kt + 1) * 65],
                                at[:, k4 * 512:(k4 + 1) * 512],
                                start=(kt == 0), stop=(kt == 15)), inc="T" if k4 == 3 else None)
                    P.set_buf(f"At{g % 2}", "T", nto)

                na_g = {}
                for g in range(4):
                    if exp_hist:
                        P.wait("T", "A", exp_hist[-1])
                    nt = mm_s_group(g)
                    if g == 0:
                        P.wait("T", "V", vp_ready)
                        P.use_bank("T", "pso")
                        if odd:
                            P.use_bank("T", "psa")
                    na_g[g] = exp_group(g, nt)
                    if g >= 1:
                        P.wait("T", "A", na_g[g - 1])
                        mm_o_group(g - 1)
                P.wait("T", "A", na_g[3])
                mm_o_group(3)
                P.wait("A", "T", P.n["T"])
                if odd:
                    P.op("A", lambda e: e.activation(rowf[0:1, 0:512], psa[0:1, :], AF.Ln))
                else:
                    P.op("A", lambda e: e.activation(rowf[0:1, 0:512], pso[64:65, :], AF.Ln))
                na = P.op("A", lambda e: e.activation(rs_row[:], rowf[0:1, 0:512], AF.Exp,
                                                      scale=-1.0), inc="A")
                if odd:
                    P.set_bank("psa", "A", na)
                P.wait("T", "A", na)
                P.use_bank("T", "psb")
                nt = P.op("T", lambda e: e.matmul(psb[:], ones_r[:], rs_row[:],
                                                  start=True, stop=True), inc="T")
                P.wait("V", "T", nt)
                P.op("V", lambda e, p0=p0: e.tensor_copy(rsb[p0:p0 + 64, :], psb[p0:p0 + 64, :]))
                P.op("V", lambda e, p0=p0: e.tensor_mul(
                    zt[p0:p0 + 64, :], pso[p0:p0 + 64, :], rsb[p0:p0 + 64, :]))
                nv = P.op("V", lambda e, tile=tile, p0=p0, c0=c0: e.tensor_add(
                    X[tile][p0:p0 + 64, c0:c0 + 512], X[tile][p0:p0 + 64, c0:c0 + 512],
                    zt[p0:p0 + 64, :]), inc="V")
                P.set_bank("pso", "V", nv)
                P.set_bank("psb", "V", nv)

    def dump_tiles(nm, tiles, wait_sem, wait_val):
        if not DEBUG:
            return
        P.wait("S", wait_sem, wait_val)
        for i, t_ in enumerate(tiles):
            P.op("S", lambda e, t_=t_, i=i, nm=nm: e.dma_start(
                out=dbg[nm][i * 128:(i + 1) * 128, :], in_=t_[:]), inc="S", dma=True)
        P.wait("V", "S", P.n["S"])

    def dump_row(nm, src, wait_sem, wait_val):
        if not DEBUG:
            return
        P.wait("S", wait_sem, wait_val)
        P.op("S", lambda e, src=src, nm=nm: e.dma_start(out=dbg[nm], in_=src[:]),
             inc="S", dma=True)
        P.wait("V", "S", P.n["S"])

    lnv = emit_ln_u(X, mu_m, rho_m, LN_EPS, wait_S=n_loadS)
    dump_tiles("d_u1", U, "V", lnv)
    P.wait("T", "G", n_loadG)
    cwv = emit_router(lnv)
    dump_row("d_cw1", cwT, "V", cwv)
    P.wait("S", "V", lnv)
    sp = None
    for i in range(NDC):
        sp = P.op("S", lambda e, i=i: e.dma_start(
            out=xsp[i * 128:(i + 1) * 128, :], in_=X[i][:]), inc="S", dma=True)
    last_mm2 = emit_ffn(sp, cwv)
    state2["ffn_calls"] = 1
    dump_tiles("d_s1", S, "V", P.n["V"])
    xu1 = emit_epilogue2(last_mm2)
    dump_tiles("d_x1", X, "V", xu1)
    lnv = emit_ln_u(X, mu_m, rho_m, LN_EPS)
    dump_tiles("d_u2", U, "V", lnv)
    emit_attention(lnv)
    dump_tiles("d_x2", X, "V", P.n["V"])
    lnv = emit_ln_u(X, mu_m, rho_m, LN_EPS)
    dump_tiles("d_u3", U, "V", lnv)
    cwv = emit_router(lnv)
    dump_row("d_cw2", cwT, "V", cwv)
    P.wait("S", "V", lnv)
    sp = None
    for i in range(NDC):
        sp = P.op("S", lambda e, i=i: e.dma_start(
            out=xsp[i * 128:(i + 1) * 128, :], in_=X[i][:]), inc="S", dma=True)
    last_mm2 = emit_ffn(sp, cwv)
    xupd = emit_epilogue2(last_mm2)
    for i in range(NDC):
        P.wait("S", "V", state2["xupd_j"][i])
        P.op("S", lambda e, i=i: e.dma_start(
            out=yt[i * 128:(i + 1) * 128, :], in_=X[i][:]), inc="S", dma=True)
    P.wait("S", "S", P.n["S"])

    with nc.Block() as block:
        @block.tensor
        def _(t):
            for f in P.ops["T"]:
                f(t)

        @block.scalar
        def _(a):
            for f in P.ops["A"]:
                f(a)

        @block.vector
        def _(v):
            for f in P.ops["V"]:
                f(v)

        @block.gpsimd
        def _(g):
            for f in P.ops["G"]:
                f(g)

        @block.sync
        def _(s):
            for f in P.ops["S"]:
                f(s)

    stack.close()
    return nc


_NC_CACHE = None


def kernel(L0_attn_inputs, L0_attn_rotation_params, L0_attn_entangle_params,
           L0_ffn_router_w, L0_ffn_router_b, L0_ffn_w1, L0_ffn_b1,
           L0_ffn_w2, L0_ffn_b2, L0_ffn_ln_w, L0_ffn_ln_b):
    global _NC_CACHE
    x = np.asarray(L0_attn_inputs, dtype=np.float32)
    rot = np.asarray(L0_attn_rotation_params, dtype=np.float32)
    ent = np.asarray(L0_attn_entangle_params, dtype=np.float32)
    wr = np.asarray(L0_ffn_router_w, dtype=np.float32)
    rb = np.asarray(L0_ffn_router_b, dtype=np.float32)
    w1 = np.asarray(L0_ffn_w1, dtype=np.float32)
    b1 = np.asarray(L0_ffn_b1, dtype=np.float32)
    w2 = np.asarray(L0_ffn_w2, dtype=np.float32)
    b2 = np.asarray(L0_ffn_b2, dtype=np.float32)
    lnw_a = np.asarray(L0_ffn_ln_w, dtype=np.float32)
    lnb_a = np.asarray(L0_ffn_ln_b, dtype=np.float32)

    w1_jit = np.ascontiguousarray(
        w1.reshape(NE, NFC, 128, NDC, 128).transpose(0, 1, 4, 3, 2)
    ).reshape(NE, NFC, 128, D)
    w2_jit = np.ascontiguousarray(
        w2.reshape(NE, NDC, 128, 2, 16, 128).transpose(0, 3, 1, 5, 4, 2)
    ).reshape(NE, 2, NDC, 128, 2048)
    b1s = np.ascontiguousarray(b1.reshape(NE, NFC, 128).transpose(2, 0, 1)).reshape(128, NE * NFC)
    b2s = np.ascontiguousarray(b2.reshape(NE, NDC, 128).transpose(2, 0, 1)).reshape(128, NE * NDC)
    wrts = np.ascontiguousarray(wr.T.reshape(NDC, 128, NE).transpose(1, 0, 2)).reshape(128, NDC * NE)
    lnw_s = np.ascontiguousarray(lnw_a.reshape(NDC, 128).T)
    lnb_s = np.ascontiguousarray(lnb_a.reshape(NDC, 128).T)
    rots_s = np.ascontiguousarray(rot.reshape(D).reshape(NDC, 128).T)
    ents_s = np.ascontiguousarray(ent.reshape(D).reshape(NDC, 128).T)
    rb_s = np.ascontiguousarray(rb.reshape(NE, 1))
    id16 = np.eye(128, dtype=ml_dtypes.bfloat16)
    id32 = np.eye(128, dtype=np.float32)
    ones_r = np.ones((1, 128), dtype=ml_dtypes.bfloat16)
    ones_c = np.ones((128, 1), dtype=ml_dtypes.bfloat16)

    if _NC_CACHE is None:
        _NC_CACHE = build_program()
    nc = _NC_CACHE

    common = {"w1j": w1_jit, "w2j": w2_jit, "b1s": b1s, "b2s": b2s,
              "wrts": wrts, "rb": rb_s, "lnw": lnw_s, "lnb": lnb_s,
              "rots": rots_s, "ents": ents_s, "id16": id16, "id32": id32,
              "ones_r": ones_r, "ones_c": ones_c,
              "epsc": np.array([[LN_EPS, FFN_LN_EPS]], dtype=np.float32)}
    in_maps = []
    for b in range(8):
        m = dict(common)
        m["xt"] = np.ascontiguousarray(x[b].T)
        in_maps.append(m)
    res = run_bass_kernel_spmd(nc, in_maps, core_ids=list(range(8)))
    out = np.empty((8, TOK, D), dtype=np.float32)
    for b in range(8):
        out[b] = res.results[b]["yt"].T
    if DEBUG:
        global DEBUG_OUT
        DEBUG_OUT = res.results
    return out
